# version 15
# baseline (speedup 1.0000x reference)
import numpy as np
from contextlib import ExitStack

import concourse.bass as bass
import concourse.mybir as mybir
from concourse.bass_utils import run_bass_kernel_spmd

F32 = mybir.dt.float32
BF16 = mybir.dt.bfloat16
AF = mybir.ActivationFunctionType
OP = mybir.AluOpType
AXX = mybir.AxisListType.X

C = 2048
T = 8192
NCORES = 8
TLOC = 1024
TP = 1026
XW = 1028
NQ = 16
NT = 3
TW = 342
TOFF = [0, 342, 684]
EPS = 1e-5
RC = 1.0 / 2048.0
NEG = -1.0e30

DEBUG_TAPS = ()
TRUNCATE = None
KEEP_DROP = ()

WSPEC = {"Wk": (16, 16), "Wv": (16, 16), "Wr": (16, 16), "Wo": (16, 16),
         "Wck": (64, 16), "Wcv": (16, 64), "Wcr": (16, 16),
         "Wgate": (16, 16), "Wup": (16, 16), "Wdown": (16, 16),
         "WlA": (1, 16), "WlB": (16, 1), "WG": (1, 16)}

VECN = ["ln1w", "ln1b", "ln2w", "ln2b", "tmk", "tmv", "tmr", "cmk", "cmr", "td", "tf"]
DVECN = ["w", "nw", "itmk", "itmv", "itmr", "icmk", "icmr"]
VS = {n: i for i, n in enumerate(VECN + DVECN)}
NVS = len(VS)

TAP_SHAPES = {
    "xk": [C, TP], "xv": [C, TP], "k": [C, TP], "v": [C, TP],
    "summ": [128, 48], "carry": [128, 48],
    "sr": [C, TP], "rwkv": [C, TP], "x1": [C, TP],
    "xk2": [C, TP], "xr2": [C, TP], "scr": [C, TP],
    "kvt": [C, TP], "x2": [C, TP], "X1": [C, TP], "z": [C, TP],
    "wts": [TP, 8], "gated": [C, TP], "fo": [C, TP], "lgt": [8, TP],
}


class Sched:
    ENGS = ["pe", "ve", "ac", "go", "sy"]

    def __init__(self, nc, es):
        self.nc = nc
        self.es = es
        self.steps = {e: [] for e in self.ENGS}
        self.ms = {}
        self.n = {e: 0 for e in self.ENGS}
        self.sem = {e: es.enter_context(nc.semaphore("sem_" + e)) for e in self.ENGS}
        self.dcls = {}
        self.ccsem = es.enter_context(nc.semaphore("ccs"))
        self.ccn = 0

    def dclass(self, name):
        if name not in self.dcls:
            self.dcls[name] = [self.es.enter_context(self.nc.semaphore("dc_" + name)), 0]
        return self.dcls[name]

    def step(self, name, eng, deps, fn, kind="i", dcls=None):
        assert name not in self.ms, name
        if name.startswith("tap"):
            if getattr(self, "_last_tap", None) is not None:
                deps = list(deps) + [self._last_tap]
            self._last_tap = name
        if kind == "d":
            assert dcls is not None, name
            c = self.dclass(dcls)
            c[1] += 16
            self.ms[name] = ("D:" + dcls, c[1])
        elif kind == "c":
            self.ccn += 1
            self.ms[name] = ("C", self.ccn)
        else:
            self.n[eng] += 1
            self.ms[name] = (eng, self.n[eng])
        self.steps[eng].append((name, fn, list(deps), kind, dcls))

    def truncate(self, n_keep, drop=()):
        order = {}
        i = 0
        for name in self.ms:
            order[name] = i
            i += 1
        keep = set(n for n in self.ms if order[n] < n_keep and n not in drop)
        changed = True
        while changed:
            changed = False
            for e in self.ENGS:
                for name, fn, deps, kind, dcls in self.steps[e]:
                    if name in keep and any(d not in keep for d in deps):
                        keep.discard(name)
                        changed = True
        self.steps = {e: [s for s in self.steps[e] if s[0] in keep]
                      for e in self.ENGS}
        self.n = {e: 0 for e in self.ENGS}
        for c in self.dcls.values():
            c[1] = 0
        self.ccn = 0
        newms = {}
        for e in self.ENGS:
            for name, fn, deps, kind, dcls in self.steps[e]:
                if kind == "d":
                    c = self.dcls[dcls]
                    c[1] += 16
                    newms[name] = ("D:" + dcls, c[1])
                elif kind == "c":
                    self.ccn += 1
                    newms[name] = ("C", self.ccn)
                else:
                    self.n[e] += 1
                    newms[name] = (e, self.n[e])
        self.ms = newms

    def emit(self, blk):
        sched = self

        def mk(eng_name):
            def body(engobj):
                waited = {}
                for name, fn, deps, kind, dcls in sched.steps[eng_name]:
                    for d in deps:
                        assert d in sched.ms, f"step {name} depends on undefined {d}"
                        de, dv = sched.ms[d]
                        if de == eng_name:
                            assert dv < sched.ms[name][1] or sched.ms[name][0] != de, \
                                f"forward same-engine dep {name} -> {d}"
                            continue
                        if de.startswith("D:"):
                            s = sched.dcls[de[2:]][0]
                            key = de
                        elif de == "C":
                            s = sched.ccsem
                            key = "C"
                        else:
                            s = sched.sem[de]
                            key = de
                        if waited.get(key, -1) >= dv:
                            continue
                        engobj.wait_ge(s, dv)
                        waited[key] = dv
                    inst = fn(engobj)
                    if kind == "d":
                        inst.then_inc(sched.dcls[dcls][0], 16)
                    elif kind == "c":
                        inst.then_inc(sched.ccsem, 1)
                    else:
                        assert inst is not None, f"step {name} returned no instruction"
                        inst.then_inc(sched.sem[eng_name], 1)
            return body

        blk.tensor(mk("pe"))
        blk.vector(mk("ve"))
        blk.scalar(mk("ac"))
        blk.gpsimd(mk("go"))
        blk.sync(mk("sy"))


def _stage_w(W):
    M, K = W.shape
    Mq, Kq = M // 128, K // 128
    s = np.asarray(W, np.float32).reshape(Mq, 128, Kq, 128).transpose(0, 3, 2, 1)
    return np.ascontiguousarray(s.reshape(Mq, 128, Kq * 128))


def _stage_vec(v):
    return np.ascontiguousarray(np.asarray(v, np.float32).reshape(16, 128).T)


def build_program(debug_taps=()):
    nc = bass.Bass("TRN2", target_bir_lowering=False, detect_race_conditions=False)
    es = ExitStack()
    g = Sched(nc, es)

    def din(name, shape, dt=F32):
        return nc.dram_tensor(name, shape, dt, kind="ExternalInput").ap()

    xf = din("xf", [C, XW])
    wd = {nm: din(nm, [mq, 128, kq * 128]) for nm, (mq, kq) in WSPEC.items()}
    vecs = din("vecs", [128, len(VECN) * 16])
    shiftsd = din("shifts", [128, 8])
    masksd = din("masks", [128, 8])
    halomd = din("halom", [128, 1])
    eyed = din("eyed", [128, 128])
    sel16d = din("sel16", [8, 128])
    selqd = din("selq", [8, NQ * 128])

    out = nc.dram_tensor("out", [C, TP], F32, kind="ExternalOutput").ap()
    taps = {}
    for tn in debug_taps:
        taps[tn] = nc.dram_tensor("tap_" + tn, TAP_SHAPES[tn], F32,
                                  kind="ExternalOutput").ap()

    cc_in = nc.dram_tensor("cc_in", [128, 48], F32)
    cc_out = nc.dram_tensor("cc_out", [NCORES, 128, 48], F32)
    sp_nd = nc.dram_tensor("sp_nd", [NQ, 3, 128, TP], BF16)
    sp_x1 = nc.dram_tensor("sp_x1", [NQ, 128, TP], BF16)
    sp_scr = nc.dram_tensor("sp_scr", [NQ, 128, TP], BF16)
    sp_X1 = nc.dram_tensor("sp_X1", [NQ, 128, TP], F32)

    def sb(name, shape, dt):
        return es.enter_context(nc.sbuf_tensor(name, shape, dt))

    vec_sb = sb("vec_sb", [128, 16 * NVS], F32)

    def vcol(n):
        return vec_sb[:, VS[n] * 16:(VS[n] + 1) * 16]

    def vch(n, q):
        return vec_sb[:, VS[n] * 16 + q:VS[n] * 16 + q + 1]

    shifts_sb = sb("shifts_sb", [128, 8], F32)
    masks_sb = sb("masks_sb", [128, 8], F32)
    halo_sb = sb("halo_sb", [128, 1], F32)
    eye_sb = sb("eye_sb", [128, 128], F32)
    ones_bf = sb("ones_bf", [128, 128], BF16)
    sel16_sb = sb("sel16_sb", [8, 128], F32)
    selq_sb = sb("selq_sb", [8, NQ * 128], F32)

    mu_full = sb("mu_full", [128, TP], BF16)
    r0_full = sb("r0_full", [128, TP], BF16)
    rq_full = sb("rq_full", [128, TP], BF16)

    BIG0 = sb("BIG0", [128, NQ * TP], BF16)
    BIG1 = sb("BIG1", [128, NQ * TP], BF16)
    BIG2 = sb("BIG2", [128, NQ * TP], BF16)

    def big(buf, q, off=0, n=TP):
        return buf[:, q * TP + off:q * TP + off + n]

    BIG2f = BIG2[:, :].bitcast(F32)

    def S(i, off=0, n=TP):
        return BIG2f[:, i * TP + off:i * TP + off + n]

    ndst = BIG2[:, 12 * TP:15 * TP]

    h_t = sb("h_t", [128, 343], F32)
    h_u = sb("h_u", [128, 343], F32)
    hlast = sb("hlast", [128, 16], F32)

    kv_sb = sb("kv_sb", [128, 4 * TP], F32)
    s_pad = sb("s_pad", [128, TP + 2], F32)
    a_pad = sb("a_pad", [128, TP + 2], F32)
    b_pad = sb("b_pad", [128, TP + 2], F32)
    pool = sb("pool", [128, 4 * TP], F32)

    def P(i, off=0, n=TP):
        return pool[:, i * TP + off:i * TP + off + n]

    cring = sb("cring", [128, TP], F32)
    xring = sb("xring", [128, 5 * 343], F32)

    def xrs(s):
        return xring[:, (s % 5) * 343:(s % 5) * 343 + 343]

    sbf_ring = sb("sbf_ring", [128, 6 * 343], BF16)

    def sbf(s):
        return sbf_ring[:, (s % 6) * 343:(s % 6) * 343 + 343]

    wstage = [sb(f"wstage{i}", [128, 2048], F32) for i in range(2)]
    wbf = [sb(f"wbf{i}", [128, 2048], BF16) for i in range(3)]

    carry_sb = sb("carry_sb", [128, 3 * 16], F32)

    def carry(q, j):
        return carry_sb[:, q * 3 + j:q * 3 + j + 1]

    summ_sb = sb("summ_sb", [128, 48], F32)
    eps_col = sb("eps_col", [128, 1], F32)
    _TBTAPS = {"xk", "xv", "sr", "rwkv", "xk2", "xr2", "kvt", "scr", "z",
               "wts", "gated", "fo", "x2", "X1", "lgt"}
    tapbuf = sb("tapbuf", [128, TP], F32) if (set(taps) & _TBTAPS) else None
    small = sb("small", [128, 160], F32)
    smbf = sb("smbf", [128, 32], BF16)
    wfm_sb = sb("wfm_sb", [8, TP], F32)

    psb = [es.enter_context(nc.psum_tensor(f"psb{i}", [128, 512], F32))
           for i in range(8)]

    def kslot(q, off=0, n=TP):
        return kv_sb[:, (q % 2) * TP + off:(q % 2) * TP + off + n]

    def vslot(q, off=0, n=TP):
        return kv_sb[:, (2 + q % 2) * TP + off:(2 + q % 2) * TP + off + n]

    kv_bf = kv_sb[:, :].bitcast(BF16)

    def ndm(q, j, off=0, n=TP):
        b0 = (q % 2) * 3 * TP + j * TP
        return kv_bf[:, b0 + off:b0 + off + n]

    padsbf = [s_pad[:, 0:TP].bitcast(BF16), a_pad[:, 0:TP].bitcast(BF16),
              b_pad[:, 0:TP].bitcast(BF16)]

    def srslot(q, off=0, n=TP):
        return padsbf[q % 3][:, off:off + n]

    BIG1f = BIG1[:, :].bitcast(F32)

    def kvsl(s_):
        if s_ < 24:
            return BIG1f[:, s_ * 342:(s_ + 1) * 342]
        if s_ < 36:
            u = s_ - 24
            return kv_sb[:, u * 342:(u + 1) * 342]
        if s_ < 45:
            u = s_ - 36
            t_ = (s_pad, a_pad, b_pad)[u // 3]
            return t_[:, (u % 3) * 342:(u % 3) * 342 + 342]
        u = s_ - 45
        return P(3, u * 342, 342)

    pool2bf = pool[:, 2 * TP:3 * TP].bitcast(BF16)
    cringbf = cring[:, :].bitcast(BF16)

    def x1back(q, off=0, n=TP):
        b0 = (q % 2) * TP
        return pool2bf[:, b0 + off:b0 + off + n]

    def scrback(q, off=0, n=TP):
        b0 = (q % 2) * TP
        return cringbf[:, b0 + off:b0 + off + n]

    def lgrow(off=0, n=TP):
        return kv_sb[0:8, off:off + n]

    def tmg(gr, i):
        return kv_sb[:, TP + gr * 8: TP + gr * 8 + 8] if i == 0 else \
            kv_sb[:, TP + 80 + gr * 8: TP + 80 + gr * 8 + 8]

    def ecol(i):
        return small[:, 64 + i:65 + i]

    GRP = [(i * 128, 128) for i in range(8)] + [(1024, 2)]
    NG = len(GRP)

    pieces = []

    def piece(nm, mq, kg=0):
        pieces.append((nm, mq, kg))

    for mq in range(16):
        piece("Wk", mq)
        piece("Wv", mq)
    for mq in range(16):
        piece("Wr", mq)
    for mq in range(16):
        piece("Wo", mq)
    for mq in range(16):
        piece("Wcr", mq)
    for fb in range(4):
        for m2 in range(16):
            piece("Wck", fb * 16 + m2)
        for mq in range(16):
            piece("Wcv", mq, fb)
    piece("WG", 0)
    for mq in range(16):
        piece("Wgate", mq)
        piece("Wup", mq)
    piece("WlA", 0)
    for mq in range(16):
        piece("Wdown", mq)
        piece("WlB", mq)
    NPC = len(pieces)
    pcols = [min(WSPEC[nm][1] * 128, 2048) for nm, _, _ in pieces]

    def wslot(i):
        return wbf[i % 3]

    def wsl(i, kq):
        return wslot(i)[:, kq * 128:(kq + 1) * 128]

    def ldn(i):
        return f"ld{i}"

    def cvn(i):
        return f"cv{i}"

    use_step = [None] * NPC

    def piece_user(i):
        if i < 32:
            return (f"Apk{i // 2}t{NT - 1}" if i % 2 == 0
                    else f"Apv{i // 2}t{NT - 1}")
        if i < 48:
            return f"Bpr{i - 32}t{NT - 1}"
        if i < 64:
            return f"Bpo{i - 48}t{NT - 1}"
        if i < 80:
            return f"Cpcr{i - 64}"
        if i < 208:
            j = i - 80
            fb, r = j // 32, j % 32
            return (f"Cpck{fb}_{r}t{NT - 1}" if r < 16
                    else f"Cpcv{fb}_{r - 16}t{NT - 1}")
        if i == 208:
            return f"Eprt{NT - 1}"
        if i < 241:
            j = i - 209
            return (f"Epg{j // 2}t{NT - 1}" if j % 2 == 0
                    else f"Epu{j // 2}t{NT - 1}")
        if i == 241:
            return f"Eplr{NT - 1}"
        j = i - 242
        return f"Epd{j // 2}t{NT - 1}"

    def emit_piece_steps(i, extra_dep=None):
        deps_ld = []
        if i >= 2:
            deps_ld.append(cvn(i - 2))
        if extra_dep:
            deps_ld.append(extra_dep)
        nm, mq, kg = pieces[i]
        ncols = pcols[i]
        ap = wd[nm][mq, :, kg * 2048:kg * 2048 + ncols]

        def ld(e, ap=ap, i=i, ncols=ncols):
            return e.dma_start(out=wstage[i % 2][:, 0:ncols], in_=ap)
        g.step(ldn(i), "sy", deps_ld, ld, kind="d", dcls=f"w{i % 2}")

        ce = ("ac", "go", "ac", "go", "ve", "go")[i % 6]
        deps_cv = [ldn(i)]
        if i >= 3:
            deps_cv.append(piece_user(i - 3))

        def cv(e, i=i, ncols=ncols, ce=ce):
            if ce == "ac":
                return e.activation(out=wslot(i)[:, 0:ncols],
                                    in_=wstage[i % 2][:, 0:ncols], func=AF.Copy)
            return e.tensor_copy(out=wslot(i)[:, 0:ncols],
                                 in_=wstage[i % 2][:, 0:ncols])
        g.step(cvn(i), ce, deps_cv, cv)

    _piece_ptr = [0]

    def ensure_pieces(upto):
        while _piece_ptr[0] <= min(upto, NPC - 1):
            emit_piece_steps(_piece_ptr[0])
            _piece_ptr[0] += 1

    def xdma_step(name, q, tau, slot, deps, eng="sy"):
        off = TOFF[tau]
        ap = xf[q * 128:(q + 1) * 128, off:off + 343]
        slot = slot % 5

        def fn(e, ap=ap, slot=slot):
            return e.dma_start(out=xrs(slot), in_=ap)
        g.step(name, eng, deps, fn, kind="d", dcls=f"x{slot}")

    _cl = [(vec_sb[:, 0:len(VECN) * 16], vecs[:, :]),
           (shifts_sb[:, :], shiftsd[:, :]), (masks_sb[:, :], masksd[:, :]),
           (halo_sb[:, :], halomd[:, :]), (eye_sb[:, :], eyed[:, :]),
           (sel16_sb[:, :], sel16d[:, :]), (selq_sb[:, :], selqd[:, :])]
    for _ci, (_dst, _src) in enumerate(_cl):
        g.step(f"consts{_ci}" if _ci < len(_cl) - 1 else "consts", "sy", [],
               (lambda e, d=_dst, s=_src: e.dma_start(out=d, in_=s)), kind="d",
               dcls="c0")

    def c_memset(e):
        e.memset(ones_bf[:, :], 1.0)
        e.memset(hlast[:, :], 0.0)
        e.memset(s_pad[:, 0:1], NEG)
        e.memset(a_pad[:, 0:1], 0.0)
        e.memset(b_pad[:, 0:1], 0.0)
        e.iota(small[:, 144:152], [[1, 8]], channel_multiplier=0,
               allow_small_or_imprecise_dtypes=True)
        e.memset(eps_col[:, :], EPS)
        return e.engine_nop()
    g.step("minit", "go", [], c_memset)

    def c_w(e):
        return e.activation(out=vcol("w"), in_=vcol("td"), func=AF.Exp)
    g.step("cw", "ac", ["consts"], c_w)

    def c_inv(e):
        for dst, srcn in [("itmk", "tmk"), ("itmv", "tmv"), ("itmr", "tmr"),
                          ("icmk", "cmk"), ("icmr", "cmr")]:
            e.tensor_scalar(out=vcol(dst), in0=vcol(srcn), scalar1=-1.0,
                            scalar2=1.0, op0=OP.mult, op1=OP.add)
        return e.tensor_scalar(out=vcol("nw"), in0=vcol("w"), scalar1=-1.0,
                               scalar2=None, op0=OP.mult)
    g.step("cinv", "ve", ["consts", "cw"], c_inv)

    def emit_halo(tag, extra_deps):
        sc = lambda i: small[:, i:i + 1]
        ap = bass.AP(xf.tensor, 0, [[XW, 128], [128 * XW, 16]])
        g.step(f"hx{tag}", "sy", extra_deps,
               lambda e: e.dma_start(out=small[:, 0:16], in_=ap,
                                     allow_slow_non_contiguous=True), kind="d",
               dcls="hx" + tag)

        def a1(e):
            e.activation(out=smbf[:, 0:16], in_=small[:, 0:16], func=AF.Copy)
            return e.activation(out=smbf[:, 16:32], in_=small[:, 0:16], func=AF.Square)
        g.step(f"ha1{tag}", "ac", [f"hx{tag}", "cw"], a1)

        def p1(e):
            e.matmul(psb[6][:, 0:16], ones_bf[:, :], smbf[:, 0:16], start=True, stop=True)
            return e.matmul(psb[7][:, 0:16], ones_bf[:, :], smbf[:, 16:32],
                            start=True, stop=True)
        g.step(f"hp{tag}", "pe", [f"ha1{tag}", "minit"], p1)

        def v1(e):
            e.tensor_reduce(out=sc(16), in_=psb[6][:, 0:16], axis=AXX, op=OP.add)
            e.tensor_reduce(out=sc(17), in_=psb[7][:, 0:16], axis=AXX, op=OP.add)
            e.drain()
            e.tensor_scalar(out=sc(16), in0=sc(16), scalar1=RC, scalar2=None, op0=OP.mult)
            e.tensor_scalar(out=sc(17), in0=sc(17), scalar1=RC, scalar2=None, op0=OP.mult)
            e.drain()
            e.tensor_tensor(out=sc(18), in0=sc(16), in1=sc(16), op=OP.mult)
            e.drain()
            return e.tensor_tensor(out=sc(18), in0=sc(17), in1=sc(18), op=OP.subtract)
        g.step(f"hv1{tag}", "ve", [f"hp{tag}", "cinv"], v1)

        g.step(f"ha2{tag}", "ac", [f"hv1{tag}"],
               lambda e: e.activation(out=sc(19), in_=sc(18), func=AF.Sqrt, bias=eps_col[:, 0:1]))

        def v2(e):
            e.reciprocal(out=sc(20), in_=sc(19))
            e.drain()
            e.tensor_tensor(out=sc(21), in0=sc(18), in1=sc(20), op=OP.mult)
            e.drain()
            return e.tensor_tensor(out=sc(21), in0=sc(21), in1=sc(20), op=OP.mult)
        g.step(f"hv2{tag}", "ve", [f"ha2{tag}"], v2)

        g.step(f"ha3{tag}", "ac", [f"hv2{tag}"],
               lambda e: e.activation(out=sc(22), in_=sc(21), func=AF.Sqrt, bias=eps_col[:, 0:1]))

        def v3(e):
            e.reciprocal(out=sc(23), in_=sc(22))
            e.drain()
            e.tensor_tensor(out=sc(23), in0=sc(23), in1=sc(20), op=OP.mult)
            e.drain()
            e.tensor_scalar(out=small[:, 32:48], in0=small[:, 0:16],
                            scalar1=sc(16), scalar2=None, op0=OP.subtract)
            e.drain()
            e.tensor_scalar(out=small[:, 32:48], in0=small[:, 32:48],
                            scalar1=sc(23), scalar2=None, op0=OP.mult)
            e.tensor_tensor(out=small[:, 32:48], in0=small[:, 32:48],
                            in1=vcol("ln1w"), op=OP.mult)
            e.tensor_tensor(out=small[:, 32:48], in0=small[:, 32:48],
                            in1=vcol("ln1b"), op=OP.add)
            return e.tensor_scalar(out=hlast[:, :], in0=small[:, 32:48],
                                   scalar1=halo_sb[:, 0:1], scalar2=None, op0=OP.mult)
        g.step(f"hl{tag}", "ve", [f"ha3{tag}"], v3)
        return f"hl{tag}"

    hA = emit_halo("A", [])

    def ATMP(i):
        return P(0 if i % 2 == 0 else 2, 342, 342)

    def APK(i):
        return P(0 if i % 2 == 0 else 2, 0, 342)

    def APV(i):
        return P(1, 0 if i % 2 == 0 else 342, 342)

    def AH(i):
        return h_t if i % 2 == 0 else h_u

    NMX = NT * NQ
    _amix_prev = [None]

    for tau in range(NT):
        off = TOFF[tau]
        for q in range(NQ):
            i = tau * NQ + q
            deps = []
            if i >= 3:
                deps.append(f"Asa{i - 3}")
            xdma_step(f"Asx{tau}_{q}", q, tau, i % 3, deps,
                      eng="go" if q % 2 else "sy")

        for q in range(NQ):
            i = tau * NQ + q

            def a(e, i=i, tau=tau, q=q):
                e.activation(out=sbf(2 * (i % 3))[:, 0:342],
                             in_=xrs(i % 3)[:, 1:343], func=AF.Copy)
                return e.activation(out=sbf(2 * (i % 3) + 1)[:, 0:342],
                                    in_=xrs(i % 3)[:, 1:343], func=AF.Square)
            deps = [f"Asx{tau}_{q}"]
            if i >= 3:
                deps.append(f"Asp{i - 3}")
            g.step(f"Asa{i}", "ac", deps, a)

        for q in range(NQ):
            i = tau * NQ + q

            def p(e, i=i, q=q):
                e.matmul(psb[6][:, 0:342], ones_bf[:, :], sbf(2 * (i % 3))[:, 0:342],
                         start=(q == 0), stop=(q == NQ - 1))
                return e.matmul(psb[7][:, 0:342], ones_bf[:, :],
                                sbf(2 * (i % 3) + 1)[:, 0:342],
                                start=(q == 0), stop=(q == NQ - 1))
            deps = [f"Asa{i}", "minit"]
            if i == 0:
                deps.append("hv1A")
            if q == 0 and tau >= 1:
                deps.append(f"Av{tau - 1}")
            g.step(f"Asp{i}", "pe", deps, p)

        def v(e, tau=tau, off=off):
            e.tensor_scalar(out=S(0, 0, 342), in0=psb[6][:, 0:342], scalar1=RC,
                            scalar2=None, op0=OP.mult)
            e.tensor_copy(out=mu_full[:, off:off + 342], in_=S(0, 0, 342))
            e.tensor_tensor(out=S(0, 342, 342), in0=S(0, 0, 342), in1=S(0, 0, 342),
                            op=OP.mult)
            return e.scalar_tensor_tensor(out=S(0, 684, 342), in0=psb[7][:, 0:342],
                                          scalar=RC, in1=S(0, 342, 342),
                                          op0=OP.mult, op1=OP.subtract)
        g.step(f"Av{tau}", "ve", [f"Asp{tau * NQ + NQ - 1}"], v)

        g.step(f"Aa{tau}", "ac", [f"Av{tau}"],
               lambda e: e.activation(out=S(1, 0, 342), in_=S(0, 684, 342),
                                      func=AF.Sqrt, bias=eps_col[:, 0:1]))

        def v2(e, tau=tau, off=off):
            e.reciprocal(out=S(1, 342, 342), in_=S(1, 0, 342))
            e.tensor_copy(out=r0_full[:, off:off + 342], in_=S(1, 342, 342))
            e.tensor_tensor(out=S(1, 684, 342), in0=S(0, 684, 342),
                            in1=S(1, 342, 342), op=OP.mult)
            return e.tensor_tensor(out=S(1, 684, 342), in0=S(1, 684, 342),
                                   in1=S(1, 342, 342), op=OP.mult)
        g.step(f"Av2{tau}", "ve", [f"Aa{tau}"], v2)

        g.step(f"Aa2{tau}", "ac", [f"Av2{tau}"],
               lambda e: e.activation(out=S(2, 0, 342), in_=S(1, 684, 342),
                                      func=AF.Sqrt, bias=eps_col[:, 0:1]))

        def v3(e, tau=tau, off=off):
            e.reciprocal(out=S(2, 342, 342), in_=S(2, 0, 342))
            return e.tensor_tensor(out=rq_full[:, off:off + 342], in0=S(2, 342, 342),
                                   in1=S(1, 342, 342), op=OP.mult)
        g.step(f"Av3{tau}", "ve", [f"Aa2{tau}"], v3)


        for q in range(NQ):
            i = tau * NQ + q
            deps = []
            if i >= 2:
                deps.append(f"Amxp{i - 2}")
            xdma_step(f"Amxd{i}", q, tau, 3 + (i % 2), deps,
                      eng="sy" if q % 2 else "go")

            def mixp(e, i=i, off=off, tau=tau):
                e.tensor_tensor(out=ATMP(i), in0=xrs(3 + i % 2)[:, 1:343],
                                in1=mu_full[:, off:off + 342], op=OP.subtract)
                return e.tensor_tensor(out=ATMP(i), in0=ATMP(i),
                                       in1=rq_full[:, off:off + 342], op=OP.mult)
            deps = [f"Amxd{i}", f"Av3{tau}"]
            if i == 0:
                deps.append(hA)
            if i >= 2:
                deps.append(f"Amxa{i - 2}")
            g.step(f"Amxp{i}", "ve", deps, mixp)

            def mixa(e, i=i, q=q):
                h = AH(i)
                e.activation(out=h[:, 0:1], in_=hlast[:, q:q + 1], func=AF.Copy)
                return e.activation(out=h[:, 1:343], in_=ATMP(i), func=AF.Identity,
                                    scale=vch("ln1w", q), bias=vch("ln1b", q))
            deps = [f"Amxp{i}"]
            if i >= 2:
                deps.append(f"Amx{i - 2}")
            g.step(f"Amxa{i}", "ac", deps, mixa)

            def mix(e, i=i, q=q, off=off):
                h = AH(i)
                e.tensor_tensor(out=P(1, 0, 342), in0=h[:, 1:343],
                                in1=h[:, 0:342], op=OP.subtract)
                e.scalar_tensor_tensor(out=big(BIG0, q, off, 342),
                                       in0=P(1, 0, 342), scalar=vch("tmk", q),
                                       in1=h[:, 0:342], op0=OP.mult, op1=OP.add)
                e.scalar_tensor_tensor(out=big(BIG1, q, off, 342),
                                       in0=P(1, 0, 342), scalar=vch("tmv", q),
                                       in1=h[:, 0:342], op0=OP.mult, op1=OP.add)
                return e.tensor_copy(out=hlast[:, q:q + 1], in_=h[:, 342:343])
            if i >= 1:
                g.step(f"Amx{i - 1}", "ve", [f"Amxa{i - 1}"], _amix_prev[0])
            _amix_prev[0] = mix
    g.step(f"Amx{NT * NQ - 1}", "ve", [f"Amxa{NT * NQ - 1}"], _amix_prev[0])
    MIXDONE = f"Amx{NT * NQ - 1}"

    for nm, bb in [("xk", BIG0), ("xv", BIG1)]:
        if nm in taps:
            for q in range(NQ):
                def tp(e, q=q, bb=bb, nm=nm):
                    e.tensor_copy(out=tapbuf[:, :], in_=big(bb, q))
                    return e.dma_start(out=taps[nm][q * 128:(q + 1) * 128, :], in_=tapbuf[:, :])
                g.step(f"tap_{nm}{q}", "go", [MIXDONE], tp, kind="d", dcls="tap")

    def EP(q):
        return P(q % 2)

    def MT(q):
        return P(1 - q % 2)

    def emit_scan_h1(q):
        def sc1(e, q=q):
            e.tensor_scalar(out=P(3), in0=kslot(q), scalar1=0.0,
                            scalar2=vch("w", q), op0=OP.mult, op1=OP.subtract)
            e.tensor_tensor_scan(out=s_pad[:, 1:TP + 1], data0=P(3),
                                 data1=kslot(q), initial=NEG,
                                 op0=OP.add, op1=OP.max)
            e.tensor_tensor(out=S(1), in0=s_pad[:, 0:TP], in1=s_pad[:, 1:TP + 1],
                            op=OP.subtract)
            return e.tensor_tensor(out=S(2), in0=kslot(q), in1=s_pad[:, 1:TP + 1],
                                   op=OP.subtract)
        deps = [f"Aek{q}"]
        if q >= 1:
            deps.append(f"Asc2b{q - 1}")
        g.step(f"Asc1_{q}", "ve", deps, sc1)

        def sce(e, q=q):
            e.activation(out=S(1), in_=S(1), func=AF.Exp, bias=vch("nw", q))
            e.activation(out=S(3), in_=S(2), func=AF.Exp)
            return e.activation(out=EP(q), in_=kslot(q), func=AF.Identity,
                                bias=vch("tf", q))
        deps = [f"Asc1_{q}"]
        if q >= 1:
            deps.append(f"Asc2b{q - 1}")
        g.step(f"Asce{q}", "ac", deps, sce)

    def emit_scan_h2a(q):
        def sc2(e, q=q):
            e.tensor_tensor(out=S(4), in0=S(3), in1=vslot(q), op=OP.mult)
            e.tensor_tensor_scan(out=a_pad[:, 1:TP + 1], data0=S(1), data1=S(4),
                                 initial=0.0, op0=OP.mult, op1=OP.add)
            e.tensor_tensor_scan(out=b_pad[:, 1:TP + 1], data0=S(1), data1=S(3),
                                 initial=0.0, op0=OP.mult, op1=OP.add)
            return e.tensor_tensor(out=MT(q), in0=s_pad[:, 0:TP], in1=EP(q),
                                   op=OP.max)
        deps = [f"Asce{q}", f"Aev{q}"]
        if q >= 1:
            deps.append(f"Asc{q - 1}")
        g.step(f"Asc2_{q}", "ve", deps, sc2)

        def sc2b(e, q=q):
            e.tensor_copy(out=ndst[:, 0:TP], in_=MT(q))
            e.tensor_tensor(out=S(0), in0=s_pad[:, 0:TP], in1=ndst[:, 0:TP],
                            op=OP.subtract)
            e.tensor_tensor(out=EP(q), in0=EP(q), in1=ndst[:, 0:TP],
                            op=OP.subtract)
            return e.tensor_copy(out=summ_sb[:, q * 3:q * 3 + 1],
                                 in_=s_pad[:, 1024:1025])
        deps = [f"Asc2_{q}"]
        if q >= 1:
            deps.append(f"Aspl{q - 1}")
        g.step(f"Asc2b{q}", "ve", deps, sc2b)

    def emit_scan_h2b(q):
        def sce2(e, q=q):
            e.activation(out=S(0), in_=S(0), func=AF.Exp)
            return e.activation(out=EP(q), in_=EP(q), func=AF.Exp)
        g.step(f"Asce2_{q}", "ac", [f"Asc2b{q}"], sce2)

        def sc4(e, q=q):
            e.tensor_tensor(out=S(4), in0=vslot(q), in1=EP(q), op=OP.mult)
            e.tensor_tensor(out=P(2), in0=a_pad[:, 0:TP], in1=S(0), op=OP.mult)
            e.tensor_tensor(out=ndst[:, TP:2 * TP], in0=P(2), in1=S(4), op=OP.add)
            e.tensor_tensor(out=P(2), in0=b_pad[:, 0:TP], in1=S(0), op=OP.mult)
            e.tensor_tensor(out=ndst[:, 2 * TP:3 * TP], in0=P(2), in1=EP(q),
                            op=OP.add)
            e.tensor_copy(out=summ_sb[:, q * 3 + 1:q * 3 + 2],
                          in_=a_pad[:, 1024:1025])
            return e.tensor_copy(out=summ_sb[:, q * 3 + 2:q * 3 + 3],
                                 in_=b_pad[:, 1024:1025])
        g.step(f"Asc{q}", "ve", [f"Asce2_{q}"], sc4)

        def spl(e, q=q):
            ap = bass.AP(sp_nd.ap().tensor, q * 3 * 128 * TP,
                         [[TP, 128], [128 * TP, 3], [1, TP]])
            return e.dma_start(out=ap,
                               in_=ndst[:, :].rearrange("p (j t) -> p j t", j=3))
        g.step(f"Aspl{q}", "sy", [f"Asc{q}"], spl, kind="d", dcls=f"spnd{q % 2}")

        for nm, slot in [("k", kslot), ("v", vslot)]:
            if nm in taps:
                def tp(e, q=q, slot=slot, nm=nm):
                    return e.dma_start(out=taps[nm][q * 128:(q + 1) * 128, :],
                                       in_=slot(q))
                g.step(f"tap_{nm}{q}", "go", [f"Asc{q}"], tp, kind="d", dcls="tap")

    ensure_pieces(3)
    for mq in range(NQ):
        ik, iv = 2 * mq, 2 * mq + 1

        for tau in range(NT):
            def pk(e, ik=ik, tau=tau):
                off = TOFF[tau]
                for kq in range(NQ):
                    last = e.matmul(psb[tau][:, 0:342], wsl(ik, kq),
                                    big(BIG0, kq, off, 342),
                                    start=(kq == 0), stop=(kq == NQ - 1))
                return last
            deps = [cvn(ik), f"Amx{tau * NQ + NQ - 1}"]
            if mq >= 1:
                deps.append(f"Aek{mq - 1}")
            g.step(f"Apk{mq}t{tau}", "pe", deps, pk)

            def pv(e, iv=iv, tau=tau):
                off = TOFF[tau]
                for kq in range(NQ):
                    last = e.matmul(psb[3 + tau][:, 0:342], wsl(iv, kq),
                                    big(BIG1, kq, off, 342),
                                    start=(kq == 0), stop=(kq == NQ - 1))
                return last
            deps = [cvn(iv), f"Amx{tau * NQ + NQ - 1}"]
            if mq >= 1:
                deps.append(f"Aev{mq - 1}")
            g.step(f"Apv{mq}t{tau}", "pe", deps, pv)
        use_step[ik] = f"Apk{mq}t{NT - 1}"
        use_step[iv] = f"Apv{mq}t{NT - 1}"

        if mq >= 2:
            emit_scan_h2a(mq - 2)
        if mq >= 1:
            emit_scan_h1(mq - 1)
        if mq >= 2:
            emit_scan_h2b(mq - 2)

        def ek(e, mq=mq):
            for tau in range(NT):
                last = e.activation(out=kslot(mq, TOFF[tau], 342),
                                    in_=psb[tau][:, 0:342], func=AF.Copy)
            return last
        deps = [f"Apk{mq}t{NT - 1}"]
        if mq >= 2:
            deps.append(f"Asce{mq - 2}")
        g.step(f"Aek{mq}", "ac", deps, ek)

        def ev(e, mq=mq):
            for tau in range(NT):
                last = e.activation(out=vslot(mq, TOFF[tau], 342),
                                    in_=psb[3 + tau][:, 0:342], func=AF.Copy)
            return last
        deps = [f"Apv{mq}t{NT - 1}"]
        if mq >= 2:
            deps.append(f"Asc{mq - 2}")
        g.step(f"Aev{mq}", "ac", deps, ev)

        ensure_pieces(min(2 * (mq + 2) + 1, 31))
    emit_scan_h2a(NQ - 2)
    emit_scan_h1(NQ - 1)
    emit_scan_h2b(NQ - 2)
    emit_scan_h2a(NQ - 1)
    emit_scan_h2b(NQ - 1)
    ensure_pieces(34)
    hB = emit_halo("B", [f"Av{NT - 1}", f"Amxa{NT * NQ - 1}"])

    def summd(e):
        return e.dma_start(out=cc_in.ap(), in_=summ_sb[:, :])
    g.step("summd", "sy", [f"Asc{NQ - 1}"], summd, kind="d", dcls="ccin")

    if "summ" in taps:
        def tpsm(e):
            return e.dma_start(out=taps["summ"][:, :], in_=summ_sb[:, :])
        g.step("tap_summ", "go", [f"Asc{NQ - 1}"], tpsm, kind="d", dcls="tap")

    def coll(e):
        return e.collective_compute(
            "AllGather", OP.bypass, replica_groups=[list(range(NCORES))],
            ins=[cc_in.ap().opt()], outs=[cc_out.ap().opt()])
    g.step("cc", "go", ["summd"], coll, kind="c")


    for tau in range(NT):
        for q in range(NQ):
            i = tau * NQ + q
            deps = [f"Apv{NQ - 1}t{NT - 1}"]
            if i >= 5:
                deps.append(f"Bxrp{i - 5}")
            xdma_step(f"Bxd{i}", q, tau, i, deps)

    def BTMP(i):
        return P(0, 342 if i % 2 == 0 else 684, 342)

    def BPK(i):
        return P(0 if i % 2 == 0 else 1, 0, 342)

    for i in range(NMX):
        tau, q = divmod(i, NQ)
        off = TOFF[tau]

        def mixrp(e, i=i, off=off, tau=tau):
            e.tensor_tensor(out=BTMP(i), in0=xrs(i)[:, 1:343],
                            in1=mu_full[:, off:off + 342], op=OP.subtract)
            return e.tensor_tensor(out=BTMP(i), in0=BTMP(i),
                                   in1=rq_full[:, off:off + 342], op=OP.mult)
        deps = [f"Bxd{i}", f"Apk{NQ - 1}t{NT - 1}"]
        if i == 0:
            deps.append(hB)
        if i >= 2:
            deps.append(f"Bxra{i - 2}")
        g.step(f"Bxrp{i}", "ve", deps, mixrp)

        def mixra(e, i=i, q=q):
            h = AH(i)
            e.activation(out=h[:, 0:1], in_=hlast[:, q:q + 1], func=AF.Copy)
            e.activation(out=h[:, 1:343], in_=BTMP(i), func=AF.Identity,
                         scale=vch("ln1w", q), bias=vch("ln1b", q))
            e.activation(out=BPK(i), in_=h[:, 1:343], func=AF.Copy,
                         scale=vch("tmr", q))
            return e.activation(out=hlast[:, q:q + 1], in_=BTMP(i)[:, 341:342],
                                func=AF.Identity, scale=vch("ln1w", q),
                                bias=vch("ln1b", q))
        deps = [f"Bxrp{i}"]
        if i >= 2:
            deps.append(f"Bxr{i - 2}")
        g.step(f"Bxra{i}", "ac", deps, mixra)

        def mixr(e, i=i, q=q, off=off):
            return e.scalar_tensor_tensor(out=big(BIG0, q, off, 342),
                                          in0=AH(i)[:, 0:342],
                                          scalar=vch("itmr", q), in1=BPK(i),
                                          op0=OP.mult, op1=OP.add)
        if i >= 1:
            g.step(f"Bxr{i - 1}", "ve", [f"Bxra{i - 1}"], mixr_prev)
        mixr_prev = mixr
    g.step(f"Bxr{NMX - 1}", "ve", [f"Bxra{NMX - 1}"], mixr_prev)
    XRDONE = f"Bxr{NT * NQ - 1}"

    for q in range(NQ):
        def gad(e, q=q):
            ap = bass.AP(cc_out.ap().tensor, q * 3,
                         [[48, 128], [128 * 48, 8], [1, 3]])
            return e.dma_start(out=P(2, q * 24, 24).rearrange("p (i j) -> p i j", j=3),
                               in_=ap)
        g.step(f"gad{q}", "sy", ["cc", f"Asc{NQ - 1}"], gad, kind="d", dcls="gad")

    def g_s(q):
        return P(2, 640 + q * 8, 8)

    def g_a(q):
        return P(2, 768 + q * 8, 8)

    def g_b(q):
        return P(2, 384 + q * 8, 8)

    def g_t(q):
        return P(2, 512 + q * 8, 8)

    def comb0(e):
        for q in range(NQ):
            v = P(2, q * 24, 24).rearrange("p (i j) -> p j i", j=3)
            e.tensor_copy(out=g_s(q), in_=v[:, 0, :])
            e.tensor_copy(out=g_a(q), in_=v[:, 1, :])
            last = e.tensor_copy(out=g_b(q), in_=v[:, 2, :])
        return last
    g.step("comb0", "ve", [f"gad{q}" for q in range(NQ)], comb0)

    def comb1(e):
        for q in range(NQ):
            e.tensor_scalar(out=g_t(q), in0=shifts_sb[:, :],
                            scalar1=vch("w", q), scalar2=None, op0=OP.mult)
        for q in range(NQ):
            e.tensor_tensor(out=g_t(q), in0=g_s(q), in1=g_t(q), op=OP.subtract)
        for q in range(NQ):
            e.tensor_reduce(out=carry(q, 2), in_=g_t(q), axis=AXX, op=OP.max)
        for q in range(NQ):
            e.tensor_scalar(out=g_t(q), in0=g_t(q),
                            scalar1=carry(q, 2), scalar2=None, op0=OP.subtract)
        return e.engine_nop()
    g.step("comb1", "ve", ["comb0"], comb1)

    def combe(e):
        for q in range(NQ):
            last = e.activation(out=g_t(q), in_=g_t(q), func=AF.Exp)
        return last
    g.step("combe", "ac", ["comb1"], combe)

    def comb2(e):
        for q in range(NQ):
            e.tensor_tensor(out=g_t(q), in0=g_t(q), in1=masks_sb[:, :], op=OP.mult)
        for q in range(NQ):
            e.tensor_tensor(out=g_a(q), in0=g_a(q), in1=g_t(q), op=OP.mult)
        for q in range(NQ):
            e.tensor_reduce(out=carry(q, 0), in_=g_a(q), axis=AXX, op=OP.add)
        for q in range(NQ):
            e.tensor_tensor(out=g_b(q), in0=g_b(q), in1=g_t(q), op=OP.mult)
        for q in range(NQ):
            last = e.tensor_reduce(out=carry(q, 1), in_=g_b(q), axis=AXX, op=OP.add)
        return last
    g.step("comb2", "ve", ["combe"], comb2)

    if "carry" in taps:
        def tpc(e):
            return e.dma_start(out=taps["carry"][:, :], in_=carry_sb[:, 0:48])
        g.step("tap_carry", "go", ["comb2"], tpc, kind="d", dcls="tap")


    g.step("Biota", "go", [],
           lambda e: e.iota(cring[:, 0:TP], [[1, TP]], channel_multiplier=0,
                            allow_small_or_imprecise_dtypes=True))

    def emit_fixup(q):
        def ndb(e, q=q):
            ap = bass.AP(sp_nd.ap().tensor, q * 3 * 128 * TP,
                         [[TP, 128], [128 * TP, 3], [1, TP]])
            b0 = (q % 2) * 3 * TP
            return e.dma_start(out=kv_bf[:, b0:b0 + 3 * TP].rearrange(
                "p (j t) -> p j t", j=3), in_=ap)
        deps = [f"Aspl{q}", f"Asc{NQ - 1}"]
        if q >= 2:
            deps.append(f"Bfx{q - 2}")
        g.step(f"Bndb{q}", "sy", deps, ndb, kind="d", dcls=f"nb{q % 2}")

        def fx1(e, q=q):
            e.tensor_scalar(out=S(0), in0=cring[:, 0:TP], scalar1=vch("nw", q),
                            scalar2=carry(q, 2), op0=OP.mult, op1=OP.add)
            e.tensor_tensor(out=S(1), in0=ndm(q, 0), in1=S(0), op=OP.max)
            e.tensor_tensor(out=S(2), in0=ndm(q, 0), in1=S(1), op=OP.subtract)
            return e.tensor_tensor(out=S(0), in0=S(0), in1=S(1), op=OP.subtract)
        deps = [f"Bndb{q}", "Biota", XRDONE, "comb2"]
        g.step(f"Bfx1_{q}", "ve", deps, fx1)

        def fxe(e, q=q):
            e.activation(out=S(2), in_=S(2), func=AF.Exp)
            return e.activation(out=S(0), in_=S(0), func=AF.Exp)
        g.step(f"Bfxe{q}", "ac", [f"Bfx1_{q}"], fxe)

        def fxg(e, q=q):
            e.tensor_tensor(out=S(3), in0=ndm(q, 1), in1=S(2), op=OP.mult)
            return e.tensor_tensor(out=S(4), in0=ndm(q, 2), in1=S(2), op=OP.mult)
        g.step(f"Bfxg{q}", "ve", [f"Bfxe{q}"], fxg)

        def fx2(e, q=q):
            e.scalar_tensor_tensor(out=S(3), in0=S(0), scalar=carry(q, 0),
                                   in1=S(3), op0=OP.mult, op1=OP.add)
            e.scalar_tensor_tensor(out=S(4), in0=S(0), scalar=carry(q, 1),
                                   in1=S(4), op0=OP.mult, op1=OP.add)
            e.reciprocal(out=S(5), in_=S(4))
            e.tensor_tensor(out=S(3), in0=S(3), in1=S(5), op=OP.mult)
            return e.tensor_tensor(out=big(BIG1, q), in0=S(3), in1=srslot(q),
                                   op=OP.mult)
        g.step(f"Bfx{q}", "ve", [f"Bfxg{q}", f"Besr{q}", f"Apv{NQ - 1}t{NT - 1}"], fx2)

        for nm in ("sr", "rwkv"):
            if nm in taps:
                def tp(e, q=q, nm=nm):
                    if nm == "sr":
                        e.tensor_copy(out=tapbuf[:, :], in_=srslot(q))
                    else:
                        e.tensor_copy(out=tapbuf[:, :], in_=big(BIG1, q))
                    return e.dma_start(out=taps[nm][q * 128:(q + 1) * 128, :], in_=tapbuf[:, :])
                g.step(f"tap_{nm}{q}", "go", [f"Bfx{q}"], tp, kind="d", dcls="tap")

    ensure_pieces(34)
    for mq in range(NQ):
        i = 32 + mq

        for tau in range(NT):
            def pr(e, i=i, tau=tau):
                off = TOFF[tau]
                for kq in range(NQ):
                    last = e.matmul(psb[tau][:, 0:342], wsl(i, kq),
                                    big(BIG0, kq, off, 342),
                                    start=(kq == 0), stop=(kq == NQ - 1))
                return last
            deps = [cvn(i), f"Bxr{tau * NQ + NQ - 1}"]
            if mq >= 1:
                deps.append(f"Besr{mq - 1}")
            g.step(f"Bpr{mq}t{tau}", "pe", deps, pr)
        use_step[i] = f"Bpr{mq}t{NT - 1}"

        def esr(e, mq=mq):
            for tau in range(NT):
                last = e.activation(out=srslot(mq, TOFF[tau], 342),
                                    in_=psb[tau][:, 0:342], func=AF.Sigmoid)
            return last
        deps = [f"Bpr{mq}t{NT - 1}"]
        if mq >= 3:
            deps.append(f"Bfx{mq - 3}")
        g.step(f"Besr{mq}", "ac", deps, esr)

        if mq >= 1:
            emit_fixup(mq - 1)
        ensure_pieces(32 + mq + 2)
    emit_fixup(NQ - 1)


    ensure_pieces(49)
    for mq in range(NQ):
        i = 48 + mq
        for tau in range(NT):
            i2 = mq * NT + tau
            deps = []
            if i2 >= 3:
                deps.append(f"Bx1_{i2 - 3}")
            else:
                deps.append(f"Bxr{45 + i2}")
            xdma_step(f"Bx2d{i2}", mq, tau, i2, deps)

        for tau in range(NT):
            i2 = mq * NT + tau

            def po(e, i=i, mq=mq, tau=tau, i2=i2):
                off = TOFF[tau]
                for kq in range(NQ):
                    last = e.matmul(psb[i2 % 2][:, 0:342], wsl(i, kq),
                                    big(BIG1, kq, off, 342),
                                    start=(kq == 0), stop=(kq == NQ - 1))
                return last
            deps = [cvn(i), f"Bfx{NQ - 1}"]
            if i2 >= 2:
                deps.append(f"Bx1_{i2 - 2}")
            g.step(f"Bpo{mq}t{tau}", "pe", deps, po)
        use_step[i] = f"Bpo{mq}t{NT - 1}"

        for tau in range(NT):
            i2 = mq * NT + tau

            def x1f(e, mq=mq, tau=tau, i2=i2):
                off = TOFF[tau]
                e.tensor_tensor(out=S(0, 0, 342), in0=xrs(i2)[:, 1:343],
                                in1=mu_full[:, off:off + 342], op=OP.subtract)
                e.tensor_tensor(out=S(0, 0, 342), in0=S(0, 0, 342),
                                in1=r0_full[:, off:off + 342], op=OP.mult)
                return e.tensor_tensor(out=sbf(i2 % 3)[:, 0:342], in0=S(0, 0, 342),
                                       in1=psb[i2 % 2][:, 0:342], op=OP.add)
            deps = [f"Bpo{mq}t{tau}", f"Bx2d{i2}", XRDONE]
            if i2 >= 3:
                deps.append(f"Bspl{i2 - 3}")
            g.step(f"Bx1_{i2}", "ve", deps, x1f)

            def sq(e, i2=i2):
                return e.activation(out=sbf(3 + (i2 % 3))[:, 0:342],
                                    in_=sbf(i2 % 3)[:, 0:342], func=AF.Square)
            deps = [f"Bx1_{i2}"]
            if i2 >= 3:
                deps.append(f"Bst{i2 - 3}")
            g.step(f"Bsq{i2}", "ac", deps, sq)

            def stat(e, mq=mq, tau=tau, i2=i2):
                e.matmul(psb[2 + tau][:, 0:342], ones_bf[:, :], sbf(i2 % 3)[:, 0:342],
                         start=(mq == 0), stop=(mq == NQ - 1))
                return e.matmul(psb[5 + tau][:, 0:342], ones_bf[:, :],
                                sbf(3 + (i2 % 3))[:, 0:342],
                                start=(mq == 0), stop=(mq == NQ - 1))
            g.step(f"Bst{i2}", "pe", [f"Bsq{i2}"], stat)

            def spl(e, mq=mq, tau=tau, i2=i2):
                ap = bass.AP(sp_x1.ap().tensor, mq * 128 * TP + TOFF[tau],
                             [[TP, 128], [1, 342]])
                return e.dma_start(out=ap, in_=sbf(i2 % 3)[:, 0:342])
            g.step(f"Bspl{i2}", "sy", [f"Bst{i2}"], spl, kind="d", dcls=f"sp{i2 % 3}")

            if "x1" in taps and tau == NT - 1:
                def tp(e, mq=mq):
                    ap = bass.AP(sp_x1.ap().tensor, mq * 128 * TP, [[TP, 128], [1, TP]])
                    return e.dma_start(out=taps["x1"][mq * 128:(mq + 1) * 128, :], in_=ap)
                g.step(f"tap_x1{i2}", "go",
                       [f"Bspl{mq * NT}", f"Bspl{mq * NT + 1}", f"Bspl{mq * NT + 2}"],
                       tp, kind="d", dcls="tap")
        if True:
            ensure_pieces(48 + mq + 2)

    for tau in range(NT):
        off = TOFF[tau]

        def ln2v(e, tau=tau, off=off):
            e.tensor_scalar(out=P(0, 0, 342), in0=psb[2 + tau][:, 0:342],
                            scalar1=RC, scalar2=None, op0=OP.mult)
            e.tensor_copy(out=mu_full[:, off:off + 342], in_=P(0, 0, 342))
            e.tensor_tensor(out=P(0, 342, 342), in0=P(0, 0, 342), in1=P(0, 0, 342),
                            op=OP.mult)
            return e.scalar_tensor_tensor(out=P(0, 684, 342), in0=psb[5 + tau][:, 0:342],
                                          scalar=RC, in1=P(0, 342, 342),
                                          op0=OP.mult, op1=OP.subtract)
        g.step(f"Bln2v{tau}", "ve", [f"Bst{(NQ - 1) * NT + NT - 1}"], ln2v)

        g.step(f"Bln2a{tau}", "ac", [f"Bln2v{tau}"],
               lambda e: e.activation(out=P(1, 0, 342), in_=P(0, 684, 342),
                                      func=AF.Sqrt, bias=eps_col[:, 0:1]))

        def ln2r(e, tau=tau, off=off):
            e.reciprocal(out=P(1, 342, 342), in_=P(1, 0, 342))
            return e.tensor_copy(out=r0_full[:, off:off + 342], in_=P(1, 342, 342))
        g.step(f"Bln2r{tau}", "ve", [f"Bln2a{tau}"], ln2r)
    LN2 = f"Bln2r{NT - 1}"


    def GH(q, off=0, n=TP + 1):
        b = 0 if q % 2 == 0 else 4 * TP
        return BIG2f[:, b + off:b + off + n]

    def GPR(q):
        b = TP + 1 if q % 2 == 0 else 5 * TP + 1
        return BIG2f[:, b:b + TP]

    def GPK(q):
        b = 2 * TP if q % 2 == 0 else 6 * TP + 1
        return BIG2f[:, b:b + TP]

    for q in range(NQ):
        def x1b(e, q=q):
            ap = bass.AP(sp_x1.ap().tensor, q * 128 * TP, [[TP, 128], [1, TP]])
            return e.dma_start(out=x1back(q), in_=ap)
        deps = [f"Bspl{q * NT}", f"Bspl{q * NT + 1}", f"Bspl{q * NT + 2}",
                "comb2"]
        if q >= 2:
            deps.append(f"Cgp{q - 2}")
        g.step(f"Cx1b{q}", "sy", deps, x1b, kind="d", dcls=f"xb{q % 2}")

        def gp(e, q=q):
            e.memset(GH(q, 0, 1), 0.0)
            e.tensor_tensor(out=GPK(q), in0=x1back(q), in1=mu_full[:, :],
                            op=OP.subtract)
            return e.tensor_tensor(out=GPK(q), in0=GPK(q), in1=r0_full[:, :],
                                   op=OP.mult)
        deps = [f"Cx1b{q}", LN2, f"Bfx{NQ - 1}"]
        if q >= 2:
            deps.append(f"Cg{q - 2}")
        g.step(f"Cgp{q}", "ve", deps, gp)

        def ga(e, q=q):
            e.activation(out=GH(q, 1, TP), in_=GPK(q), func=AF.Identity,
                         scale=vch("ln2w", q), bias=vch("ln2b", q))
            e.activation(out=GPK(q), in_=GH(q, 1, TP), func=AF.Copy,
                         scale=vch("cmk", q))
            return e.activation(out=GPR(q), in_=GH(q, 1, TP), func=AF.Copy,
                                scale=vch("cmr", q))
        deps = [f"Cgp{q}"]
        if q >= 2:
            deps.append(f"Cg{q - 2}")
        g.step(f"Cga{q}", "ac", deps, ga)

        def gf(e, q=q):
            e.scalar_tensor_tensor(out=big(BIG0, q), in0=GH(q, 0, TP),
                                   scalar=vch("icmk", q), in1=GPK(q),
                                   op0=OP.mult, op1=OP.add)
            return e.scalar_tensor_tensor(out=big(BIG1, q), in0=GH(q, 0, TP),
                                          scalar=vch("icmr", q), in1=GPR(q),
                                          op0=OP.mult, op1=OP.add)
        if q >= 1:
            g.step(f"Cg{q - 1}", "ve",
                   [f"Cga{q - 1}", f"Bpr{NQ - 1}t{NT - 1}",
                    f"Bpo{NQ - 1}t{NT - 1}"], gf_prev)
        gf_prev = gf
    g.step(f"Cg{NQ - 1}", "ve",
           [f"Cga{NQ - 1}", f"Bpr{NQ - 1}t{NT - 1}",
            f"Bpo{NQ - 1}t{NT - 1}"], gf_prev)
    GDONE = f"Cg{NQ - 1}"

    for nm, bb in [("xk2", BIG0), ("xr2", BIG1)]:
        if nm in taps:
            for q in range(NQ):
                def tp(e, q=q, bb=bb, nm=nm):
                    e.tensor_copy(out=tapbuf[:, :], in_=big(bb, q))
                    return e.dma_start(out=taps[nm][q * 128:(q + 1) * 128, :], in_=tapbuf[:, :])
                g.step(f"tap_{nm}{q}", "go", [GDONE], tp, kind="d", dcls="tap")

    ensure_pieces(65)
    for mq in range(NQ):
        i = 64 + mq

        def pcr(e, i=i, mq=mq):
            for tau in range(NT):
                off = TOFF[tau]
                for kq in range(NQ):
                    last = e.matmul(psb[tau][:, 0:342], wsl(i, kq),
                                    big(BIG1, kq, off, 342),
                                    start=(kq == 0), stop=(kq == NQ - 1))
            return last
        deps = [cvn(i), GDONE]
        if mq >= 1:
            deps.append(f"Cescr{mq - 1}")
        g.step(f"Cpcr{mq}", "pe", deps, pcr)
        use_step[i] = f"Cpcr{mq}"

        def escr(e, mq=mq):
            for tau in range(NT):
                last = e.activation(out=scrback(mq, TOFF[tau], 342),
                                    in_=psb[tau][:, 0:342], func=AF.Sigmoid)
            return last
        deps = [f"Cpcr{mq}"]
        if mq >= 2:
            deps.append(f"Cscspl{mq - 2}")
        g.step(f"Cescr{mq}", "ac", deps, escr)

        def scspl(e, mq=mq):
            ap = bass.AP(sp_scr.ap().tensor, mq * 128 * TP, [[TP, 128], [1, TP]])
            return e.dma_start(out=ap, in_=scrback(mq))
        g.step(f"Cscspl{mq}", "sy", [f"Cescr{mq}"], scspl, kind="d", dcls=f"ss{mq % 2}")

        if "scr" in taps:
            def tp(e, mq=mq):
                ap = bass.AP(sp_scr.ap().tensor, mq * 128 * TP, [[TP, 128], [1, TP]])
                return e.dma_start(out=taps["scr"][mq * 128:(mq + 1) * 128, :], in_=ap)
            g.step(f"tap_scr{mq}", "go", [f"Cscspl{mq}"], tp, kind="d", dcls="tap")
        ensure_pieces(64 + mq + 2)

    for fb in range(4):
        base = 80 + fb * 32
        for m2 in range(16):
            i = base + m2

            for tau in range(NT):
                i3 = m2 * NT + tau

                def pck(e, i=i, m2=m2, tau=tau, i3=i3):
                    off = TOFF[tau]
                    for kq in range(NQ):
                        last = e.matmul(psb[i3 % 2][:, 0:342], wsl(i, kq),
                                        big(BIG0, kq, off, 342),
                                        start=(kq == 0), stop=(kq == NQ - 1))
                    return last
                deps = [cvn(i), GDONE, f"Cescr{NQ - 1}"]
                if i3 >= 2:
                    deps.append(f"Ckf{fb}_{i3 - 2}")
                elif fb >= 1:
                    deps.append(f"Ckf{fb - 1}_{15 * NT + NT - 1}")
                g.step(f"Cpck{fb}_{m2}t{tau}", "pe", deps, pck)

                def kfe(e, m2=m2, tau=tau, i3=i3, fb=fb):
                    b = i3 % 2
                    e.activation(out=P(0, 0, 342), in_=psb[b][:, 0:342], func=AF.Relu)
                    return e.activation(out=big(BIG2, m2, TOFF[tau], 342),
                                        in_=P(0, 0, 342), func=AF.Square)
                deps = [f"Cpck{fb}_{m2}t{tau}"]
                if fb >= 1:
                    deps.append(f"Ckv{fb - 1}_{NQ - 1}_{NT - 1}")
                g.step(f"Ckf{fb}_{i3}", "ac", deps, kfe)
            use_step[i] = f"Cpck{fb}_{m2}t{NT - 1}"
            ensure_pieces(i + 2)

        for mq in range(16):
            i = base + 16 + mq

            for tau in range(NT):
                i2 = mq * NT + tau

                def pcv(e, i=i, mq=mq, fb=fb, tau=tau, i2=i2):
                    off = TOFF[tau]
                    for kq in range(NQ):
                        last = e.matmul(psb[2 + i2 % 2][:, 0:342], wsl(i, kq),
                                        big(BIG2, kq, off, 342),
                                        start=(kq == 0), stop=(kq == NQ - 1))
                    return last
                deps = [cvn(i), f"Ckf{fb}_{15 * NT + NT - 1}"]
                if i2 >= 2:
                    deps.append(f"Ckv{fb}_{(i2 - 2) // NT}_{(i2 - 2) % NT}")
                g.step(f"Cpcv{fb}_{mq}t{tau}", "pe", deps, pcv)

                def kva(e, mq=mq, tau=tau, fb=fb, i2=i2):
                    b = 2 + i2 % 2
                    s_ = mq * NT + tau
                    if fb == 0:
                        return e.tensor_copy(out=kvsl(s_), in_=psb[b][:, 0:342])
                    return e.tensor_tensor(out=kvsl(s_), in0=kvsl(s_),
                                           in1=psb[b][:, 0:342], op=OP.add)
                deps = [f"Cpcv{fb}_{mq}t{tau}"]
                if fb == 0:
                    deps += [f"Bpo{NQ - 1}t{NT - 1}", GDONE]
                g.step(f"Ckv{fb}_{mq}_{tau}", "ve", deps, kva)
            use_step[i] = f"Cpcv{fb}_{mq}t{NT - 1}"
            ensure_pieces(i + 2)

    KVDONE = f"Ckv3_{NQ - 1}_{NT - 1}"

    if "kvt" in taps:
        for q in range(NQ):
            def tp(e, q=q):
                for tau in range(NT):
                    e.tensor_copy(out=tapbuf[:, TOFF[tau]:TOFF[tau]+342], in_=kvsl(q * NT + tau))
                return e.dma_start(out=taps["kvt"][q * 128:(q + 1) * 128, :], in_=tapbuf[:, :])
            g.step(f"tap_kvt{q}", "go", [KVDONE], tp, kind="d", dcls="tap")

    for mq in range(NQ):
        def x1b2(e, mq=mq):
            ap = bass.AP(sp_x1.ap().tensor, mq * 128 * TP, [[TP, 128], [1, TP]])
            return e.dma_start(out=x1back(mq), in_=ap)
        deps = [KVDONE]
        if mq >= 2:
            deps.append(f"Cx2_{mq - 2}_{NT - 1}")
        g.step(f"Cx1c{mq}", "sy", deps, x1b2, kind="d", dcls=f"xb{mq % 2}")

        def scrb(e, mq=mq):
            ap = bass.AP(sp_scr.ap().tensor, mq * 128 * TP, [[TP, 128], [1, TP]])
            return e.dma_start(out=scrback(mq), in_=ap)
        deps = [f"Cscspl{mq}", KVDONE]
        if mq >= 2:
            deps.append(f"Cx2_{mq - 2}_{NT - 1}")
        g.step(f"Cscrb{mq}", "sy", deps, scrb, kind="d", dcls=f"sb{mq % 2}")

        for tau in range(NT):
            def x2f(e, mq=mq, tau=tau):
                s_ = mq * NT + tau
                off = TOFF[tau]
                e.tensor_tensor(out=kvsl(s_), in0=kvsl(s_),
                                in1=scrback(mq, off, 342), op=OP.mult)
                return e.tensor_tensor(out=kvsl(s_), in0=kvsl(s_),
                                       in1=x1back(mq, off, 342), op=OP.add)
            g.step(f"Cx2_{mq}_{tau}", "ve", [f"Cx1c{mq}", f"Cscrb{mq}"], x2f)

            def x2t(e, mq=mq, tau=tau):
                s_ = mq * NT + tau
                e.activation(out=sbf(s_ % 3)[:, 0:342], in_=kvsl(s_), func=AF.Copy)
                return e.activation(out=sbf(3 + s_ % 3)[:, 0:342], in_=kvsl(s_),
                                    func=AF.Square)
            deps = [f"Cx2_{mq}_{tau}"]
            if mq * NT + tau >= 3:
                deps.append(f"Cst{mq * NT + tau - 3}")
            g.step(f"Cx2t{mq * NT + tau}", "ac", deps, x2t)

            def st(e, mq=mq, tau=tau):
                s_ = mq * NT + tau
                e.matmul(psb[2 + tau][:, 0:342], ones_bf[:, :], sbf(s_ % 3)[:, 0:342],
                         start=(mq == 0), stop=(mq == NQ - 1))
                return e.matmul(psb[5 + tau][:, 0:342], ones_bf[:, :],
                                sbf(3 + s_ % 3)[:, 0:342],
                                start=(mq == 0), stop=(mq == NQ - 1))
            deps = [f"Cx2t{mq * NT + tau}"]
            if mq == 0:
                deps.append(f"Cpcv3_{NQ - 1}t{NT - 1}")
            g.step(f"Cst{mq * NT + tau}", "pe", deps, st)

            if "x2" in taps:
                def tp(e, mq=mq, tau=tau):
                    s_ = mq * NT + tau
                    e.tensor_copy(out=tapbuf[:, 0:342], in_=kvsl(s_))
                    return e.dma_start(out=taps["x2"][mq * 128:(mq + 1) * 128,
                                                      TOFF[tau]:TOFF[tau] + 342],
                                       in_=tapbuf[:, 0:342])
                g.step(f"tap_x2_{mq}_{tau}", "go", [f"Cx2_{mq}_{tau}"], tp,
                       kind="d", dcls="tap")

    for tau in range(NT):
        def l3v(e, tau=tau):
            e.tensor_scalar(out=P(0, tau * 342, 342), in0=psb[2 + tau][:, 0:342],
                            scalar1=RC, scalar2=None, op0=OP.mult)
            e.tensor_tensor(out=P(2, 0, 342), in0=P(0, tau * 342, 342),
                            in1=P(0, tau * 342, 342), op=OP.mult)
            return e.scalar_tensor_tensor(out=P(2, 342, 342), in0=psb[5 + tau][:, 0:342],
                                          scalar=RC, in1=P(2, 0, 342),
                                          op0=OP.mult, op1=OP.subtract)
        g.step(f"Cl3v{tau}", "ve", [f"Cst{(NQ - 1) * NT + tau}"], l3v)

        g.step(f"Cl3a{tau}", "ac", [f"Cl3v{tau}"],
               lambda e: e.activation(out=P(2, 684, 342), in_=P(2, 342, 342),
                                      func=AF.Sqrt, bias=eps_col[:, 0:1]))

        def l3r(e, tau=tau):
            return e.reciprocal(out=P(1, tau * 342, 342), in_=P(2, 684, 342))
        g.step(f"Cl3r{tau}", "ve", [f"Cl3a{tau}"], l3r)
    LN3 = f"Cl3r{NT - 1}"

    for mq in range(NQ):
        for tau in range(NT):
            i4 = mq * NT + tau
            deps = [LN3]
            if i4 >= 3:
                deps.append(f"CX1_{i4 - 3}")
            xdma_step(f"Cxd{i4}", mq, tau, i4, deps)

            def X1f(e, mq=mq, tau=tau, i4=i4):
                s_ = mq * NT + tau
                e.tensor_tensor(out=S(2, 0, 342), in0=kvsl(s_),
                                in1=P(0, tau * 342, 342), op=OP.subtract)
                e.tensor_tensor(out=S(2, 0, 342), in0=S(2, 0, 342),
                                in1=P(1, tau * 342, 342), op=OP.mult)
                return e.tensor_tensor(out=S(2, 342, 342), in0=S(2, 0, 342),
                                       in1=xrs(i4)[:, 1:343], op=OP.add)
            g.step(f"CX1_{i4}", "ve", [f"Cxd{i4}", LN3], X1f)

            def X1s(e, mq=mq, tau=tau):
                ap = bass.AP(sp_X1.ap().tensor, (mq * 128 * TP + TOFF[tau]) if False
                             else mq * 128 * TP + TOFF[tau], [[TP, 128], [1, 342]])
                return e.dma_start(out=ap, in_=S(2, 342, 342))
            g.step(f"CX1s{i4}", "sy", [f"CX1_{i4}"], X1s, kind="d", dcls="spX")

            def X1t(e, i4=i4):
                e.activation(out=sbf(i4 % 3)[:, 0:342], in_=S(2, 342, 342), func=AF.Copy)
                return e.activation(out=sbf(3 + i4 % 3)[:, 0:342], in_=S(2, 342, 342),
                                    func=AF.Square)
            deps = [f"CX1_{i4}", f"CX1s{i4}"]
            if i4 >= 3:
                deps.append(f"Cst4_{i4 - 3}")
            g.step(f"CX1t{i4}", "ac", deps, X1t)

            def st4(e, mq=mq, tau=tau, i4=i4):
                e.matmul(psb[2 + tau][:, 0:342], ones_bf[:, :], sbf(i4 % 3)[:, 0:342],
                         start=(mq == 0), stop=(mq == NQ - 1))
                return e.matmul(psb[5 + tau][:, 0:342], ones_bf[:, :],
                                sbf(3 + i4 % 3)[:, 0:342],
                                start=(mq == 0), stop=(mq == NQ - 1))
            deps = [f"CX1t{i4}"]
            if mq == 0:
                deps.append(f"Cl3v{tau}")
            g.step(f"Cst4_{i4}", "pe", deps, st4)

    for snm, fn, deps, kind, _dc in g.steps["ve"]:
        if snm.startswith("CX1_"):
            i4 = int(snm[4:])
            if i4 >= 1 and f"CX1s{i4 - 1}" in g.ms:
                deps.append(f"CX1s{i4 - 1}")

    if "X1" in taps:
        for mq in range(NQ):
            def tp(e, mq=mq):
                ap = bass.AP(sp_X1.ap().tensor, mq * 128 * TP, [[TP, 128], [1, TP]])
                return e.dma_start(out=taps["X1"][mq * 128:(mq + 1) * 128, :], in_=ap)
            g.step(f"tap_X1{mq}", "go", [f"CX1s{mq * NT + NT - 1}", KVDONE], tp,
                   kind="d", dcls="tap")

    for tau in range(NT):
        def l4v(e, tau=tau):
            e.tensor_scalar(out=P(0, tau * 342, 342), in0=psb[2 + tau][:, 0:342],
                            scalar1=RC, scalar2=None, op0=OP.mult)
            e.tensor_tensor(out=P(2, 0, 342), in0=P(0, tau * 342, 342),
                            in1=P(0, tau * 342, 342), op=OP.mult)
            return e.scalar_tensor_tensor(out=P(2, 342, 342), in0=psb[5 + tau][:, 0:342],
                                          scalar=RC, in1=P(2, 0, 342),
                                          op0=OP.mult, op1=OP.subtract)
        g.step(f"Cl4v{tau}", "ve", [f"Cst4_{(NQ - 1) * NT + tau}"], l4v)

        g.step(f"Cl4a{tau}", "ac", [f"Cl4v{tau}"],
               lambda e: e.activation(out=P(2, 684, 342), in_=P(2, 342, 342),
                                      func=AF.Sqrt, bias=eps_col[:, 0:1]))

        def l4r(e, tau=tau):
            return e.reciprocal(out=P(1, tau * 342, 342), in_=P(2, 684, 342))
        g.step(f"Cl4r{tau}", "ve", [f"Cl4a{tau}"], l4r)
    LN4 = f"Cl4r{NT - 1}"

    def zring(q, off=0, n=TP):
        b0 = (q % 2) * TP
        return kv_sb[:, b0 + off:b0 + off + n]

    for mq in range(NQ):
        def X1b3(e, mq=mq):
            ap = bass.AP(sp_X1.ap().tensor, mq * 128 * TP, [[TP, 128], [1, TP]])
            return e.dma_start(out=zring(mq), in_=ap)
        deps = [LN4]
        if mq >= 2:
            deps.append(f"Cz{mq - 2}")
        g.step(f"CX1b{mq}", "sy", deps, X1b3, kind="d", dcls=f"z{mq % 2}")

        def zf(e, mq=mq):
            for tau in range(NT):
                off = TOFF[tau]
                e.tensor_tensor(out=S(2, 0, 342), in0=zring(mq, off, 342),
                                in1=P(0, tau * 342, 342), op=OP.subtract)
                last = e.tensor_tensor(out=big(BIG0, mq, off, 342), in0=S(2, 0, 342),
                                       in1=P(1, tau * 342, 342), op=OP.mult)
            return last
        g.step(f"Cz{mq}", "ve", [f"CX1b{mq}", LN4, f"Cpck3_15t{NT - 1}"], zf)
    ZDONE = f"Cz{NQ - 1}"

    if "z" in taps:
        for q in range(NQ):
            def tp(e, q=q):
                e.tensor_copy(out=tapbuf[:, :], in_=big(BIG0, q))
                return e.dma_start(out=taps["z"][q * 128:(q + 1) * 128, :], in_=tapbuf[:, :])
            g.step(f"tap_z{q}", "go", [ZDONE], tp, kind="d", dcls="tap")

    IWG = 208
    ensure_pieces(IWG)

    def mk_prt(tau):
        def fn(e):
            off = TOFF[tau]
            for kq in range(NQ):
                last = e.matmul(psb[7][:, 0:342], wsl(IWG, kq),
                                big(BIG0, kq, off, 342),
                                start=(kq == 0), stop=(kq == NQ - 1))
            return last
        return fn
    for tau in range(NT):
        deps = [cvn(IWG), ZDONE]
        if tau >= 1:
            deps.append(f"Elge{tau - 1}")
        g.step(f"Eprt{tau}", "pe", deps, mk_prt(tau))
        g.step(f"Elge{tau}", "ac", [f"Eprt{tau}"],
               lambda e, tau=tau: e.activation(out=lgrow(TOFF[tau], 342),
                                               in_=psb[7][0:8, 0:342], func=AF.Copy))
    use_step[IWG] = f"Eprt{NT - 1}"

    if "lgt" in taps:
        def tplg(e):
            return e.dma_start(out=taps["lgt"][:, :], in_=lgrow(0, TP))
        g.step("tap_lgt", "go", [f"Elge{NT - 1}"], tplg, kind="d", dcls="tap")

    for gr in range(NG):
        g0, gn = GRP[gr]

        def ptr(e, gr=gr, g0=g0, gn=gn):
            return e.transpose(psb[6][0:gn, 0:8], lgrow(g0, gn), eye_sb[0:8, 0:8])
        deps = [f"Elge{NT - 1}"]
        if gr >= 1:
            deps.append(f"Etm{gr - 1}")
        g.step(f"Etr{gr}", "pe", deps, ptr)

        def tmc(e, gr=gr, g0=g0, gn=gn):
            return e.tensor_copy(out=tmg(gr, 0)[0:gn, :], in_=psb[6][0:gn, 0:8])
        g.step(f"Etm{gr}", "ve", [f"Etr{gr}"], tmc)

    def sm1(e):
        for gr in range(NG):
            gn = GRP[gr][1]
            e.tensor_reduce(out=ecol(gr)[0:gn, :], in_=tmg(gr, 0)[0:gn, :], axis=AXX, op=OP.max)
        for gr in range(NG):
            gn = GRP[gr][1]
            e.tensor_scalar(out=ecol(16 + gr)[0:gn, :], in0=ecol(gr)[0:gn, :],
                            scalar1=-1.0, scalar2=None, op0=OP.mult)
        return e.engine_nop()
    g.step("Esm1", "ve", [f"Etm{NG - 1}"], sm1)

    def sme(e):
        for gr in range(NG):
            gn = GRP[gr][1]
            last = e.activation(out=tmg(gr, 1)[0:gn, :], in_=tmg(gr, 0)[0:gn, :],
                                func=AF.Exp, bias=ecol(16 + gr)[0:gn, :])
        return last
    g.step("Esme", "ac", ["Esm1"], sme)

    def sm2(e):
        for gr in range(NG):
            gn = GRP[gr][1]
            e.tensor_reduce(out=ecol(gr)[0:gn, :], in_=tmg(gr, 1)[0:gn, :], axis=AXX, op=OP.add)
        for gr in range(NG):
            gn = GRP[gr][1]
            e.reciprocal(out=ecol(16 + gr)[0:gn, :], in_=ecol(gr)[0:gn, :])
        for gr in range(NG):
            gn = GRP[gr][1]
            e.tensor_scalar(out=tmg(gr, 0)[0:gn, :], in0=tmg(gr, 1)[0:gn, :],
                            scalar1=ecol(16 + gr)[0:gn, :], scalar2=None, op0=OP.mult)
        for gr in range(NG):
            gn = GRP[gr][1]
            e.tensor_reduce(out=ecol(gr)[0:gn, :], in_=tmg(gr, 0)[0:gn, :], axis=AXX, op=OP.max)
        for gr in range(NG):
            gn = GRP[gr][1]
            e.tensor_scalar(out=tmg(gr, 1)[0:gn, :], in0=tmg(gr, 0)[0:gn, :],
                            scalar1=ecol(gr)[0:gn, :], scalar2=None, op0=OP.is_ge)
        for gr in range(NG):
            gn = GRP[gr][1]
            e.tensor_scalar(out=tmg(gr, 1)[0:gn, :], in0=tmg(gr, 1)[0:gn, :],
                            scalar1=-999.0, scalar2=999.0, op0=OP.mult, op1=OP.add)
        for gr in range(NG):
            gn = GRP[gr][1]
            e.tensor_tensor(out=tmg(gr, 1)[0:gn, :], in0=tmg(gr, 1)[0:gn, :],
                            in1=small[0:gn, 144:152], op=OP.add)
        for gr in range(NG):
            gn = GRP[gr][1]
            e.tensor_reduce(out=ecol(gr)[0:gn, :], in_=tmg(gr, 1)[0:gn, :], axis=AXX, op=OP.min)
        for gr in range(NG):
            gn = GRP[gr][1]
            e.tensor_scalar(out=tmg(gr, 1)[0:gn, :], in0=small[0:gn, 144:152],
                            scalar1=ecol(gr)[0:gn, :], scalar2=None, op0=OP.is_equal)
        for gr in range(NG):
            gn = GRP[gr][1]
            e.scalar_tensor_tensor(out=tmg(gr, 0)[0:gn, :] if False else
                                   kv_sb[0:gn, 2 * TP + gr * 8:2 * TP + gr * 8 + 8],
                                   in0=tmg(gr, 1)[0:gn, :], scalar=-1e30,
                                   in1=tmg(gr, 0)[0:gn, :], op0=OP.mult, op1=OP.add)
        for gr in range(NG):
            gn = GRP[gr][1]
            e.tensor_reduce(out=ecol(gr)[0:gn, :], in_=kv_sb[0:gn, 2 * TP + gr * 8:
                                                    2 * TP + gr * 8 + 8],
                            axis=AXX, op=OP.max)
        for gr in range(NG):
            gn = GRP[gr][1]
            e.tensor_scalar(out=kv_sb[0:gn, 2 * TP + gr * 8:2 * TP + gr * 8 + 8],
                            in0=kv_sb[0:gn, 2 * TP + gr * 8:2 * TP + gr * 8 + 8],
                            scalar1=ecol(gr)[0:gn, :], scalar2=None, op0=OP.is_ge)
        for gr in range(NG):
            gn = GRP[gr][1]
            e.tensor_tensor(out=tmg(gr, 1)[0:gn, :], in0=tmg(gr, 1)[0:gn, :],
                            in1=kv_sb[0:gn, 2 * TP + gr * 8:2 * TP + gr * 8 + 8],
                            op=OP.add)
        for gr in range(NG):
            gn = GRP[gr][1]
            last = e.tensor_tensor(out=tmg(gr, 1)[0:gn, :], in0=tmg(gr, 1)[0:gn, :],
                                   in1=tmg(gr, 0)[0:gn, :], op=OP.mult)
        return last
    g.step("Esm2", "ve", ["Esme"], sm2)

    if "wts" in taps:
        for gr in range(NG):
            g0, gn = GRP[gr]

            def tp(e, gr=gr, g0=g0, gn=gn):
                e.tensor_copy(out=tapbuf[:, 0:8][0:gn, :], in_=tmg(gr, 1)[0:gn, :])
                return e.dma_start(out=taps["wts"][g0:g0 + gn, :], in_=tapbuf[:, 0:8][0:gn, :])
            g.step(f"tap_wts{gr}", "go", ["Esm2"], tp, kind="d", dcls="tap")

    for gr in range(NG):
        g0, gn = GRP[gr]

        def ptw(e, gr=gr, g0=g0, gn=gn):
            return e.transpose(psb[6][0:8, 0:gn], tmg(gr, 1)[0:gn, 0:8],
                               eye_sb[0:gn, 0:gn])
        deps = ["Esm2"]
        if gr >= 1:
            deps.append(f"Ewf{gr - 1}")
        g.step(f"Etw{gr}", "pe", deps, ptw)

        def ewf(e, gr=gr, g0=g0, gn=gn):
            return e.tensor_copy(out=wfm_sb[0:8, g0:g0 + gn], in_=psb[6][0:8, 0:gn])
        g.step(f"Ewf{gr}", "ve", [f"Etw{gr}"], ewf)
    WFM = f"Ewf{NG - 1}"

    ensure_pieces(IWG + 2)
    for mq in range(NQ):
        ig, iu = IWG + 1 + 2 * mq, IWG + 2 + 2 * mq

        for tau in range(NT):
            i5 = mq * NT + tau

            def pg(e, ig=ig, mq=mq, tau=tau, i5=i5):
                off = TOFF[tau]
                for kq in range(NQ):
                    last = e.matmul(psb[i5 % 2][:, 0:342], wsl(ig, kq),
                                    big(BIG0, kq, off, 342),
                                    start=(kq == 0), stop=(kq == NQ - 1))
                return last
            deps = [cvn(ig), ZDONE]
            if i5 >= 2:
                deps.append(f"Egs{i5 - 2}")
            g.step(f"Epg{mq}t{tau}", "pe", deps, pg)

            def eg(e, i5=i5):
                b = i5 % 2
                return e.activation(out=sbf(i5 % 3)[:, 0:342], in_=psb[b][:, 0:342],
                                    func=AF.Silu)
            deps = [f"Epg{mq}t{tau}"]
            if i5 >= 3:
                deps.append(f"Egt{i5 - 3}")
            g.step(f"Egs{i5}", "ac", deps, eg)
        use_step[ig] = f"Epg{mq}t{NT - 1}"

        for tau in range(NT):
            i5 = mq * NT + tau

            def pu(e, iu=iu, mq=mq, tau=tau, i5=i5):
                off = TOFF[tau]
                for kq in range(NQ):
                    last = e.matmul(psb[2 + i5 % 2][:, 0:342], wsl(iu, kq),
                                    big(BIG0, kq, off, 342),
                                    start=(kq == 0), stop=(kq == NQ - 1))
                return last
            deps = [cvn(iu), ZDONE]
            if i5 >= 2:
                deps.append(f"Egt{i5 - 2}")
            g.step(f"Epu{mq}t{tau}", "pe", deps, pu)

            def pwb(e, mq=mq, tau=tau, i5=i5):
                off = TOFF[tau]
                wb_b = 4 if i5 % 2 == 0 else 6
                return e.matmul(psb[wb_b][:, 0:342], selq_sb[:, mq * 128:(mq + 1) * 128],
                                wfm_sb[0:8, off:off + 342], start=True, stop=True)
            deps = [WFM, f"Epu{mq}t{tau}"]
            if i5 >= 2:
                deps.append(f"Egt{i5 - 2}")
            g.step(f"Ewb{i5}", "pe", deps, pwb)

            def gt(e, mq=mq, tau=tau, i5=i5):
                off = TOFF[tau]
                b = 2 + i5 % 2
                wb_b = 4 if i5 % 2 == 0 else 6
                e.tensor_tensor(out=big(BIG1, mq, off, 342), in0=psb[b][:, 0:342],
                                in1=sbf(i5 % 3)[:, 0:342], op=OP.mult)
                return e.tensor_tensor(out=big(BIG1, mq, off, 342),
                                       in0=big(BIG1, mq, off, 342),
                                       in1=psb[wb_b][:, 0:342], op=OP.mult)
            g.step(f"Egt{i5}", "ve", [f"Egs{i5}", f"Ewb{i5}", KVDONE], gt)
        use_step[iu] = f"Epu{mq}t{NT - 1}"
        ensure_pieces(IWG + 2 + 2 * mq + 2)
    GATED = f"Egt{(NQ - 1) * NT + NT - 1}"

    if "gated" in taps:
        for q in range(NQ):
            def tp(e, q=q):
                e.tensor_copy(out=tapbuf[:, :], in_=big(BIG1, q))
                return e.dma_start(out=taps["gated"][q * 128:(q + 1) * 128, :], in_=tapbuf[:, :])
            g.step(f"tap_gated{q}", "go", [GATED], tp, kind="d", dcls="tap")

    ILA = IWG + 33
    ensure_pieces(ILA)

    for tau in range(NT):
        def plr(e, tau=tau):
            off = TOFF[tau]
            for kq in range(NQ):
                last = e.matmul(psb[5][:, 0:342], wsl(ILA, kq),
                                big(BIG0, kq, off, 342),
                                start=(kq == 0), stop=(kq == NQ - 1))
            return last
        deps = [cvn(ILA), ZDONE]
        if tau >= 1:
            deps.append(f"Elr{tau - 1}")
        g.step(f"Eplr{tau}", "pe", deps, plr)

        def pw16(e, tau=tau):
            off = TOFF[tau]
            return e.matmul(psb[6][:, 0:342], sel16_sb[:, :],
                            wfm_sb[0:8, off:off + 342], start=True, stop=True)
        deps = [WFM, f"Etw{NG-1}", GATED]
        if tau >= 1:
            deps.append(f"Elr{tau - 1}")
        g.step(f"Ew16{tau}", "pe", deps, pw16)

        def lrf(e, tau=tau):
            off = TOFF[tau]
            e.tensor_copy(out=S(3, off, 342), in_=psb[6][:, 0:342])
            return e.tensor_tensor(out=BIG2[:, 15 * TP + off:15 * TP + off + 342],
                                   in0=S(3, off, 342), in1=psb[5][:, 0:342],
                                   op=OP.mult)
        g.step(f"Elr{tau}", "ve", [f"Eplr{tau}", f"Ew16{tau}", KVDONE], lrf)
    use_step[ILA] = f"Eplr{NT - 1}"
    LRB = f"Elr{NT - 1}"
    lrb_bf = BIG2[:, 15 * TP:16 * TP]

    ensure_pieces(ILA + 2)
    for mq in range(NQ):
        idn, ib = ILA + 1 + 2 * mq, ILA + 2 + 2 * mq

        for tau in range(NT):
            i6 = mq * NT + tau

            def pd(e, idn=idn, ib=ib, mq=mq, tau=tau, i6=i6):
                off = TOFF[tau]
                b = i6 % 2
                for kq in range(NQ):
                    e.matmul(psb[b][:, 0:342], wsl(idn, kq), big(BIG1, kq, off, 342),
                             start=(kq == 0), stop=False)
                last = e.matmul(psb[b][:, 0:342], wslot(ib)[:, 0:128],
                                lrb_bf[:, off:off + 342], start=False, stop=True)
                return last
            deps = [cvn(idn), cvn(ib), GATED, LRB]
            if i6 >= 2:
                deps.append(f"Efo{i6 - 2}")
            g.step(f"Epd{mq}t{tau}", "pe", deps, pd)
        use_step[idn] = f"Epd{mq}t{NT - 1}"
        use_step[ib] = f"Epd{mq}t{NT - 1}"

        for tau in range(NT):
            i6 = mq * NT + tau

            def efo(e, mq=mq, tau=tau, i6=i6):
                b = i6 % 2
                e.activation(out=big(BIG2, mq, TOFF[tau], 342), in_=psb[b][:, 0:342],
                             func=AF.Copy)
                return e.activation(out=sbf(3 + i6 % 3)[:, 0:342], in_=psb[b][:, 0:342],
                                    func=AF.Square)
            deps = [f"Epd{mq}t{tau}"]
            if i6 >= 3:
                deps.append(f"Est5_{i6 - 3}")
            if mq == 0 and tau == 0:
                deps.append(LRB)
            g.step(f"Efo{i6}", "ac", deps, efo)

            def st5(e, mq=mq, tau=tau, i6=i6):
                e.matmul(psb[2 + tau][:, 0:342], ones_bf[:, :],
                         big(BIG2, mq, TOFF[tau], 342),
                         start=(mq == 0), stop=(mq == NQ - 1))
                return e.matmul(psb[5 + tau][:, 0:342] if tau < 3 else None,
                                ones_bf[:, :], sbf(3 + i6 % 3)[:, 0:342],
                                start=(mq == 0), stop=(mq == NQ - 1))
            deps = [f"Efo{i6}"]
            if mq == 0:
                deps.append(f"Cst4_{(NQ - 1) * NT + tau}")
            g.step(f"Est5_{i6}", "pe", deps, st5)
        if True:
            ensure_pieces(ILA + 2 + 2 * mq + 2)


    if "fo" in taps:
        for q in range(NQ):
            def tp(e, q=q):
                e.tensor_copy(out=tapbuf[:, :], in_=big(BIG2, q))
                return e.dma_start(out=taps["fo"][q * 128:(q + 1) * 128, :], in_=tapbuf[:, :])
            g.step(f"tap_fo{q}", "go", [f"Est5_{(NQ - 1) * NT + NT - 1}"], tp,
                   kind="d", dcls="tap")

    for tau in range(NT):
        def l5v(e, tau=tau):
            e.tensor_scalar(out=P(0, tau * 342, 342), in0=psb[2 + tau][:, 0:342],
                            scalar1=RC, scalar2=None, op0=OP.mult)
            e.tensor_tensor(out=P(2, 0, 342), in0=P(0, tau * 342, 342),
                            in1=P(0, tau * 342, 342), op=OP.mult)
            return e.scalar_tensor_tensor(out=P(2, 342, 342), in0=psb[5 + tau][:, 0:342],
                                          scalar=RC, in1=P(2, 0, 342),
                                          op0=OP.mult, op1=OP.subtract)
        g.step(f"El5v{tau}", "ve", [f"Est5_{(NQ - 1) * NT + tau}"], l5v)

        g.step(f"El5a{tau}", "ac", [f"El5v{tau}"],
               lambda e: e.activation(out=P(2, 684, 342), in_=P(2, 342, 342),
                                      func=AF.Sqrt, bias=eps_col[:, 0:1]))

        def l5r(e, tau=tau):
            return e.reciprocal(out=P(1, tau * 342, 342), in_=P(2, 684, 342))
        g.step(f"El5r{tau}", "ve", [f"El5a{tau}"], l5r)
    LN5 = f"El5r{NT - 1}"

    for mq in range(NQ):
        def X1b4(e, mq=mq):
            ap = bass.AP(sp_X1.ap().tensor, mq * 128 * TP, [[TP, 128], [1, TP]])
            return e.dma_start(out=zring(mq), in_=ap)
        deps = [LN5]
        if mq >= 2:
            deps.append(f"EX2_{mq - 2}")
        g.step(f"EX1b{mq}", "sy", deps, X1b4, kind="d", dcls=f"z{mq % 2}")

        def X2f(e, mq=mq):
            for tau in range(NT):
                off = TOFF[tau]
                e.tensor_tensor(out=P(3, off, 342), in0=big(BIG2, mq, off, 342),
                                in1=P(0, tau * 342, 342), op=OP.subtract)
                e.tensor_tensor(out=P(3, off, 342), in0=P(3, off, 342),
                                in1=P(1, tau * 342, 342), op=OP.mult)
                last = e.tensor_tensor(out=P(3, off, 342), in0=P(3, off, 342),
                                       in1=zring(mq, off, 342), op=OP.add)
            return last
        deps = [f"EX1b{mq}", LN5]
        if mq >= 1:
            deps.append(f"Eout{mq - 1}")
        g.step(f"EX2_{mq}", "ve", deps, X2f)

        def outd(e, mq=mq):
            return e.dma_start(out=out[mq * 128:(mq + 1) * 128, :], in_=P(3, 0, TP))
        g.step(f"Eout{mq}", "go", [f"EX2_{mq}"], outd, kind="d", dcls="outd")

    def fin(e):
        for cname in sorted(g.dcls):
            sem_, tot = g.dcls[cname]
            if tot:
                e.wait_ge(sem_, tot)
        return e.engine_nop()
    g.step("fin", "go", [], fin)

    if TRUNCATE is not None or KEEP_DROP:
        g.truncate(TRUNCATE if TRUNCATE is not None else 10 ** 9, KEEP_DROP)
    blk = es.enter_context(nc.Block())
    g.emit(blk)
    build_program._sb = {t.tensor.name if hasattr(t, "tensor") else None: t
                         for t in []}
    build_program._handles = dict(wbf0=wbf[0], wbf1=wbf[1], wbf2=wbf[2], wstage0=wstage[0], wstage1=wstage[1], kv_sb_=kv_sb, mu_full=mu_full, r0_full=r0_full, rq_full=rq_full, h_t=h_t, xring=xring,
                                  BIG0=BIG0, BIG1=BIG1, BIG2=BIG2, hlast=hlast,
                                  kv_sb=kv_sb, s_pad=s_pad, a_pad=a_pad, b_pad=b_pad,
                                  summ_sb=summ_sb, carry_sb=carry_sb, pool=pool,
                                  vec_sb=vec_sb, wfm_sb=wfm_sb, ndst=ndst)
    es.close()
    return nc, taps


def _stage_inputs(inputs):
    p = {k[8:] if k.startswith("L0_attn_") else k: np.asarray(v)
         for k, v in inputs.items()}
    x2d = np.asarray(p["x"][0], np.float32)
    xT = np.ascontiguousarray(x2d.T)

    staged = {}
    staged["Wk"] = _stage_w(p["Wk"])
    staged["Wv"] = _stage_w(p["Wv"])
    staged["Wr"] = _stage_w(p["Wr"])
    staged["Wo"] = _stage_w(p["Wo"])
    staged["Wck"] = _stage_w(p["Wck"])
    staged["Wcv"] = _stage_w(p["Wcv"])
    staged["Wcr"] = _stage_w(p["Wcr"])
    staged["Wgate"] = _stage_w(np.asarray(p["L0_ffn_gate_w"]).reshape(2048, 2048))
    staged["Wup"] = _stage_w(np.asarray(p["L0_ffn_up_w"]).reshape(2048, 2048))
    staged["Wdown"] = _stage_w(np.asarray(p["L0_ffn_down_w"]).transpose(1, 0, 2)
                               .reshape(2048, 2048))
    staged["WlA"] = _stage_w(np.asarray(p["L0_ffn_lora_A"]).reshape(128, 2048))
    staged["WlB"] = _stage_w(np.asarray(p["L0_ffn_lora_B"]).transpose(1, 0, 2)
                             .reshape(2048, 128))
    WGp = np.zeros((128, 2048), np.float32)
    WGp[0:8, :] = np.asarray(p["L0_ffn_G"], np.float32).T
    staged["WG"] = _stage_w(WGp)

    vec_list = [p["ln1_w"], p["ln1_b"], p["ln2_w"], p["ln2_b"],
                np.asarray(p["tm_k"]).ravel(), np.asarray(p["tm_v"]).ravel(),
                np.asarray(p["tm_r"]).ravel(), np.asarray(p["cm_k"]).ravel(),
                np.asarray(p["cm_r"]).ravel(), p["time_decay"], p["time_first"]]
    vecs = np.concatenate([_stage_vec(v) for v in vec_list], axis=1)

    eye = np.eye(128, dtype=np.float32)
    sel16 = np.zeros((8, 128), np.float32)
    for e_ in range(8):
        sel16[e_, e_ * 16:(e_ + 1) * 16] = 1.0
    selq = np.zeros((8, NQ * 128), np.float32)
    for q in range(NQ):
        selq[q // 2, q * 128:(q + 1) * 128] = 1.0

    in_maps = []
    for c in range(NCORES):
        t0 = c * TLOC
        xs = np.zeros((C, XW), np.float32)
        lo = t0 - 1
        hi = t0 + XW - 1
        slo, shi = max(lo, 0), min(hi, T)
        xs[:, slo - lo:shi - lo] = xT[:, slo:shi]
        sh = np.zeros((128, 8), np.float32)
        mk = np.zeros((128, 8), np.float32)
        for i in range(8):
            if i < c:
                sh[:, i] = 1024.0 * (c - 1 - i)
                mk[:, i] = 1.0
            else:
                sh[:, i] = 1.0e30
        hm = np.full((128, 1), 1.0 if c > 0 else 0.0, np.float32)
        m = {"xf": xs, "vecs": vecs, "shifts": sh, "masks": mk, "halom": hm,
             "eyed": eye, "sel16": sel16, "selq": selq}
        m.update(staged)
        in_maps.append(m)
    return in_maps


_PROG_CACHE = {}


def kernel(**inputs):
    key = (tuple(sorted(DEBUG_TAPS)), TRUNCATE)
    if key not in _PROG_CACHE:
        _PROG_CACHE[key] = build_program(DEBUG_TAPS)
    nc, taps = _PROG_CACHE[key]
    in_maps = _stage_inputs(inputs)
    res = run_bass_kernel_spmd(nc, in_maps, core_ids=list(range(NCORES)))
    full = np.empty((T, C), np.float32)
    for c in range(NCORES):
        t0 = c * TLOC
        oc = np.asarray(res.results[c]["out"])
        j0 = 0 if c == 0 else 1
        j1 = min(TP, T - t0)
        if j1 > 1025:
            j1 = 1025
        full[t0 + j0:t0 + j1, :] = oc[:, j0:j1].T
    kernel._last = res
    return full.reshape(1, T, C)


# version 16 (final)
# speedup vs baseline: 1.0172x; 1.0172x over previous
import numpy as np
from contextlib import ExitStack

import concourse.bass as bass
import concourse.mybir as mybir
from concourse.bass_utils import run_bass_kernel_spmd

F32 = mybir.dt.float32
BF16 = mybir.dt.bfloat16
AF = mybir.ActivationFunctionType
OP = mybir.AluOpType
AXX = mybir.AxisListType.X

C = 2048
T = 8192
NCORES = 8
TLOC = 1024
TP = 1026
XW = 1028
NQ = 16
NT = 3
TW = 342
TOFF = [0, 342, 684]
EPS = 1e-5
RC = 1.0 / 2048.0
NEG = -1.0e30

DEBUG_TAPS = ()
TRUNCATE = None
KEEP_DROP = ()

WSPEC = {"Wk": (16, 16), "Wv": (16, 16), "Wr": (16, 16), "Wo": (16, 16),
         "Wck": (64, 16), "Wcv": (16, 64), "Wcr": (16, 16),
         "Wgate": (16, 16), "Wup": (16, 16), "Wdown": (16, 16),
         "WlA": (1, 16), "WlB": (16, 1), "WG": (1, 16)}

VECN = ["ln1w", "ln1b", "ln2w", "ln2b", "tmk", "tmv", "tmr", "cmk", "cmr", "td", "tf"]
DVECN = ["w", "nw", "itmk", "itmv", "itmr", "icmk", "icmr"]
VS = {n: i for i, n in enumerate(VECN + DVECN)}
NVS = len(VS)

TAP_SHAPES = {
    "xk": [C, TP], "xv": [C, TP], "k": [C, TP], "v": [C, TP],
    "summ": [128, 48], "carry": [128, 48],
    "sr": [C, TP], "rwkv": [C, TP], "x1": [C, TP],
    "xk2": [C, TP], "xr2": [C, TP], "scr": [C, TP],
    "kvt": [C, TP], "x2": [C, TP], "X1": [C, TP], "z": [C, TP],
    "wts": [TP, 8], "gated": [C, TP], "fo": [C, TP], "lgt": [8, TP],
}


class Sched:
    ENGS = ["pe", "ve", "ac", "go", "sy"]

    def __init__(self, nc, es):
        self.nc = nc
        self.es = es
        self.steps = {e: [] for e in self.ENGS}
        self.ms = {}
        self.n = {e: 0 for e in self.ENGS}
        self.sem = {e: es.enter_context(nc.semaphore("sem_" + e)) for e in self.ENGS}
        self.dcls = {}
        self.ccsem = es.enter_context(nc.semaphore("ccs"))
        self.ccn = 0

    def dclass(self, name):
        if name not in self.dcls:
            self.dcls[name] = [self.es.enter_context(self.nc.semaphore("dc_" + name)), 0]
        return self.dcls[name]

    def step(self, name, eng, deps, fn, kind="i", dcls=None):
        assert name not in self.ms, name
        if name.startswith("tap"):
            if getattr(self, "_last_tap", None) is not None:
                deps = list(deps) + [self._last_tap]
            self._last_tap = name
        if kind == "d":
            assert dcls is not None, name
            c = self.dclass(dcls)
            c[1] += 16
            self.ms[name] = ("D:" + dcls, c[1])
        elif kind == "c":
            self.ccn += 1
            self.ms[name] = ("C", self.ccn)
        else:
            self.n[eng] += 1
            self.ms[name] = (eng, self.n[eng])
        self.steps[eng].append((name, fn, list(deps), kind, dcls))

    def truncate(self, n_keep, drop=()):
        order = {}
        i = 0
        for name in self.ms:
            order[name] = i
            i += 1
        keep = set(n for n in self.ms if order[n] < n_keep and n not in drop)
        changed = True
        while changed:
            changed = False
            for e in self.ENGS:
                for name, fn, deps, kind, dcls in self.steps[e]:
                    if name in keep and any(d not in keep for d in deps):
                        keep.discard(name)
                        changed = True
        self.steps = {e: [s for s in self.steps[e] if s[0] in keep]
                      for e in self.ENGS}
        self.n = {e: 0 for e in self.ENGS}
        for c in self.dcls.values():
            c[1] = 0
        self.ccn = 0
        newms = {}
        for e in self.ENGS:
            for name, fn, deps, kind, dcls in self.steps[e]:
                if kind == "d":
                    c = self.dcls[dcls]
                    c[1] += 16
                    newms[name] = ("D:" + dcls, c[1])
                elif kind == "c":
                    self.ccn += 1
                    newms[name] = ("C", self.ccn)
                else:
                    self.n[e] += 1
                    newms[name] = (e, self.n[e])
        self.ms = newms

    def emit(self, blk):
        sched = self

        def mk(eng_name):
            def body(engobj):
                waited = {}
                for name, fn, deps, kind, dcls in sched.steps[eng_name]:
                    for d in deps:
                        assert d in sched.ms, f"step {name} depends on undefined {d}"
                        de, dv = sched.ms[d]
                        if de == eng_name:
                            assert dv < sched.ms[name][1] or sched.ms[name][0] != de, \
                                f"forward same-engine dep {name} -> {d}"
                            continue
                        if de.startswith("D:"):
                            s = sched.dcls[de[2:]][0]
                            key = de
                        elif de == "C":
                            s = sched.ccsem
                            key = "C"
                        else:
                            s = sched.sem[de]
                            key = de
                        if waited.get(key, -1) >= dv:
                            continue
                        engobj.wait_ge(s, dv)
                        waited[key] = dv
                    inst = fn(engobj)
                    if kind == "d":
                        inst.then_inc(sched.dcls[dcls][0], 16)
                    elif kind == "c":
                        inst.then_inc(sched.ccsem, 1)
                    else:
                        assert inst is not None, f"step {name} returned no instruction"
                        inst.then_inc(sched.sem[eng_name], 1)
            return body

        blk.tensor(mk("pe"))
        blk.vector(mk("ve"))
        blk.scalar(mk("ac"))
        blk.gpsimd(mk("go"))
        blk.sync(mk("sy"))


def _stage_w(W):
    M, K = W.shape
    Mq, Kq = M // 128, K // 128
    s = np.asarray(W, np.float32).reshape(Mq, 128, Kq, 128).transpose(0, 3, 2, 1)
    return np.ascontiguousarray(s.reshape(Mq, 128, Kq * 128))


def _stage_vec(v):
    return np.ascontiguousarray(np.asarray(v, np.float32).reshape(16, 128).T)


def build_program(debug_taps=()):
    nc = bass.Bass("TRN2", target_bir_lowering=False, detect_race_conditions=False)
    es = ExitStack()
    g = Sched(nc, es)

    def din(name, shape, dt=F32):
        return nc.dram_tensor(name, shape, dt, kind="ExternalInput").ap()

    xf = din("xf", [C, XW])
    wd = {nm: din(nm, [mq, 128, kq * 128]) for nm, (mq, kq) in WSPEC.items()}
    vecs = din("vecs", [128, len(VECN) * 16])
    shiftsd = din("shifts", [128, 8])
    masksd = din("masks", [128, 8])
    halomd = din("halom", [128, 1])
    eyed = din("eyed", [128, 128])
    sel16d = din("sel16", [8, 128])
    selqd = din("selq", [8, NQ * 128])

    out = nc.dram_tensor("out", [C, TP], F32, kind="ExternalOutput").ap()
    taps = {}
    for tn in debug_taps:
        taps[tn] = nc.dram_tensor("tap_" + tn, TAP_SHAPES[tn], F32,
                                  kind="ExternalOutput").ap()

    cc_in = nc.dram_tensor("cc_in", [128, 48], F32)
    cc_out = nc.dram_tensor("cc_out", [NCORES, 128, 48], F32)
    sp_nd = nc.dram_tensor("sp_nd", [NQ, 3, 128, TP], BF16)
    sp_x1 = nc.dram_tensor("sp_x1", [NQ, 128, TP], BF16)
    sp_scr = nc.dram_tensor("sp_scr", [NQ, 128, TP], BF16)
    sp_X1 = nc.dram_tensor("sp_X1", [NQ, 128, TP], F32)

    def sb(name, shape, dt):
        return es.enter_context(nc.sbuf_tensor(name, shape, dt))

    vec_sb = sb("vec_sb", [128, 16 * NVS], F32)

    def vcol(n):
        return vec_sb[:, VS[n] * 16:(VS[n] + 1) * 16]

    def vch(n, q):
        return vec_sb[:, VS[n] * 16 + q:VS[n] * 16 + q + 1]

    shifts_sb = sb("shifts_sb", [128, 8], F32)
    masks_sb = sb("masks_sb", [128, 8], F32)
    halo_sb = sb("halo_sb", [128, 1], F32)
    eye_sb = sb("eye_sb", [128, 128], F32)
    ones_bf = sb("ones_bf", [128, 128], BF16)
    sel16_sb = sb("sel16_sb", [8, 128], F32)
    selq_sb = sb("selq_sb", [8, NQ * 128], F32)

    mu_full = sb("mu_full", [128, TP], BF16)
    r0_full = sb("r0_full", [128, TP], BF16)
    rq_full = sb("rq_full", [128, TP], BF16)

    BIG0 = sb("BIG0", [128, NQ * TP], BF16)
    BIG1 = sb("BIG1", [128, NQ * TP], BF16)
    BIG2 = sb("BIG2", [128, NQ * TP], BF16)

    def big(buf, q, off=0, n=TP):
        return buf[:, q * TP + off:q * TP + off + n]

    BIG2f = BIG2[:, :].bitcast(F32)

    def S(i, off=0, n=TP):
        return BIG2f[:, i * TP + off:i * TP + off + n]

    ndst = BIG2[:, 12 * TP:15 * TP]

    h_t = sb("h_t", [128, 343], F32)
    h_u = sb("h_u", [128, 343], F32)
    hlast = sb("hlast", [128, 16], F32)

    kv_sb = sb("kv_sb", [128, 4 * TP], F32)
    s_pad = sb("s_pad", [128, TP + 2], F32)
    a_pad = sb("a_pad", [128, TP + 2], F32)
    b_pad = sb("b_pad", [128, TP + 2], F32)
    pool = sb("pool", [128, 4 * TP], F32)

    def P(i, off=0, n=TP):
        return pool[:, i * TP + off:i * TP + off + n]

    cring = sb("cring", [128, TP], F32)
    xring = sb("xring", [128, 5 * 343], F32)

    def xrs(s):
        return xring[:, (s % 5) * 343:(s % 5) * 343 + 343]

    sbf_ring = sb("sbf_ring", [128, 6 * 343], BF16)

    def sbf(s):
        return sbf_ring[:, (s % 6) * 343:(s % 6) * 343 + 343]

    wstage = [sb(f"wstage{i}", [128, 2048], F32) for i in range(2)]
    wbf = [sb(f"wbf{i}", [128, 2048], BF16) for i in range(3)]

    carry_sb = sb("carry_sb", [128, 3 * 16], F32)

    def carry(q, j):
        return carry_sb[:, q * 3 + j:q * 3 + j + 1]

    summ_sb = sb("summ_sb", [128, 48], F32)
    eps_col = sb("eps_col", [128, 1], F32)
    _TBTAPS = {"xk", "xv", "sr", "rwkv", "xk2", "xr2", "kvt", "scr", "z",
               "wts", "gated", "fo", "x2", "X1", "lgt"}
    tapbuf = sb("tapbuf", [128, TP], F32) if (set(taps) & _TBTAPS) else None
    small = sb("small", [128, 160], F32)
    smbf = sb("smbf", [128, 32], BF16)
    wfm_sb = sb("wfm_sb", [8, TP], F32)

    psb = [es.enter_context(nc.psum_tensor(f"psb{i}", [128, 512], F32))
           for i in range(8)]

    def kslot(q, off=0, n=TP):
        return kv_sb[:, (q % 2) * TP + off:(q % 2) * TP + off + n]

    def vslot(q, off=0, n=TP):
        return kv_sb[:, (2 + q % 2) * TP + off:(2 + q % 2) * TP + off + n]

    kv_bf = kv_sb[:, :].bitcast(BF16)

    def ndm(q, j, off=0, n=TP):
        b0 = (q % 2) * 3 * TP + j * TP
        return kv_bf[:, b0 + off:b0 + off + n]

    padsbf = [s_pad[:, 0:TP].bitcast(BF16), a_pad[:, 0:TP].bitcast(BF16),
              b_pad[:, 0:TP].bitcast(BF16)]

    def srslot(q, off=0, n=TP):
        return padsbf[q % 3][:, off:off + n]

    BIG1f = BIG1[:, :].bitcast(F32)

    def kvsl(s_):
        if s_ < 24:
            return BIG1f[:, s_ * 342:(s_ + 1) * 342]
        if s_ < 36:
            u = s_ - 24
            return kv_sb[:, u * 342:(u + 1) * 342]
        if s_ < 45:
            u = s_ - 36
            t_ = (s_pad, a_pad, b_pad)[u // 3]
            return t_[:, (u % 3) * 342:(u % 3) * 342 + 342]
        u = s_ - 45
        return P(3, u * 342, 342)

    pool2bf = pool[:, 2 * TP:3 * TP].bitcast(BF16)
    cringbf = cring[:, :].bitcast(BF16)

    def x1back(q, off=0, n=TP):
        b0 = (q % 2) * TP
        return pool2bf[:, b0 + off:b0 + off + n]

    def scrback(q, off=0, n=TP):
        b0 = (q % 2) * TP
        return cringbf[:, b0 + off:b0 + off + n]

    def lgrow(off=0, n=TP):
        return kv_sb[0:8, off:off + n]

    def tmg(gr, i):
        return kv_sb[:, TP + gr * 8: TP + gr * 8 + 8] if i == 0 else \
            kv_sb[:, TP + 80 + gr * 8: TP + 80 + gr * 8 + 8]

    def ecol(i):
        return small[:, 64 + i:65 + i]

    GRP = [(i * 128, 128) for i in range(8)] + [(1024, 2)]
    NG = len(GRP)

    pieces = []

    def piece(nm, mq, kg=0):
        pieces.append((nm, mq, kg))

    for mq in range(16):
        piece("Wk", mq)
        piece("Wv", mq)
    for mq in range(16):
        piece("Wr", mq)
    for mq in range(16):
        piece("Wo", mq)
    for mq in range(16):
        piece("Wcr", mq)
    for fb in range(4):
        for m2 in range(16):
            piece("Wck", fb * 16 + m2)
        for mq in range(16):
            piece("Wcv", mq, fb)
    piece("WG", 0)
    for mq in range(16):
        piece("Wgate", mq)
        piece("Wup", mq)
    piece("WlA", 0)
    for mq in range(16):
        piece("Wdown", mq)
        piece("WlB", mq)
    NPC = len(pieces)
    pcols = [min(WSPEC[nm][1] * 128, 2048) for nm, _, _ in pieces]

    def wslot(i):
        return wbf[i % 3]

    def wsl(i, kq):
        return wslot(i)[:, kq * 128:(kq + 1) * 128]

    def ldn(i):
        return f"ld{i}"

    def cvn(i):
        return f"cv{i}"

    use_step = [None] * NPC

    def piece_user(i):
        if i < 32:
            return (f"Apk{i // 2}t{NT - 1}" if i % 2 == 0
                    else f"Apv{i // 2}t{NT - 1}")
        if i < 48:
            return f"Bpr{i - 32}t{NT - 1}"
        if i < 64:
            return f"Bpo{i - 48}t{NT - 1}"
        if i < 80:
            return f"Cpcr{i - 64}"
        if i < 208:
            j = i - 80
            fb, r = j // 32, j % 32
            return (f"Cpck{fb}_{r}t{NT - 1}" if r < 16
                    else f"Cpcv{fb}_{r - 16}t{NT - 1}")
        if i == 208:
            return f"Eprt{NT - 1}"
        if i < 241:
            j = i - 209
            return (f"Epg{j // 2}t{NT - 1}" if j % 2 == 0
                    else f"Epu{j // 2}t{NT - 1}")
        if i == 241:
            return f"Eplr{NT - 1}"
        j = i - 242
        return f"Epd{j // 2}t{NT - 1}"

    def emit_piece_steps(i, extra_dep=None):
        deps_ld = []
        if i >= 2:
            deps_ld.append(cvn(i - 2))
        if extra_dep:
            deps_ld.append(extra_dep)
        nm, mq, kg = pieces[i]
        ncols = pcols[i]
        ap = wd[nm][mq, :, kg * 2048:kg * 2048 + ncols]

        def ld(e, ap=ap, i=i, ncols=ncols):
            return e.dma_start(out=wstage[i % 2][:, 0:ncols], in_=ap)
        g.step(ldn(i), "sy", deps_ld, ld, kind="d", dcls=f"w{i % 2}")

        ce = ("ac", "go", "ac", "go", "ve", "go")[i % 6]
        deps_cv = [ldn(i)]
        if i >= 3:
            deps_cv.append(piece_user(i - 3))

        def cv(e, i=i, ncols=ncols, ce=ce):
            if ce == "ac":
                return e.activation(out=wslot(i)[:, 0:ncols],
                                    in_=wstage[i % 2][:, 0:ncols], func=AF.Copy)
            return e.tensor_copy(out=wslot(i)[:, 0:ncols],
                                 in_=wstage[i % 2][:, 0:ncols])
        g.step(cvn(i), ce, deps_cv, cv)

    _piece_ptr = [0]

    def ensure_pieces(upto):
        while _piece_ptr[0] <= min(upto, NPC - 1):
            emit_piece_steps(_piece_ptr[0])
            _piece_ptr[0] += 1

    def xdma_step(name, q, tau, slot, deps, eng="sy"):
        off = TOFF[tau]
        ap = xf[q * 128:(q + 1) * 128, off:off + 343]
        slot = slot % 5

        def fn(e, ap=ap, slot=slot):
            return e.dma_start(out=xrs(slot), in_=ap)
        g.step(name, eng, deps, fn, kind="d",
               dcls=f"x{slot}" + ("g" if eng == "go" else ""))

    _cl = [(vec_sb[:, 0:len(VECN) * 16], vecs[:, :]),
           (shifts_sb[:, :], shiftsd[:, :]), (masks_sb[:, :], masksd[:, :]),
           (halo_sb[:, :], halomd[:, :]), (eye_sb[:, :], eyed[:, :]),
           (sel16_sb[:, :], sel16d[:, :]), (selq_sb[:, :], selqd[:, :])]
    for _ci, (_dst, _src) in enumerate(_cl):
        g.step(f"consts{_ci}" if _ci < len(_cl) - 1 else "consts", "sy", [],
               (lambda e, d=_dst, s=_src: e.dma_start(out=d, in_=s)), kind="d",
               dcls="c0")

    def c_memset(e):
        e.memset(ones_bf[:, :], 1.0)
        e.memset(hlast[:, :], 0.0)
        e.memset(s_pad[:, 0:1], NEG)
        e.memset(a_pad[:, 0:1], 0.0)
        e.memset(b_pad[:, 0:1], 0.0)
        e.iota(small[:, 144:152], [[1, 8]], channel_multiplier=0,
               allow_small_or_imprecise_dtypes=True)
        e.memset(eps_col[:, :], EPS)
        return e.engine_nop()
    g.step("minit", "go", [], c_memset)

    def c_w(e):
        return e.activation(out=vcol("w"), in_=vcol("td"), func=AF.Exp)
    g.step("cw", "ac", ["consts"], c_w)

    def c_inv(e):
        for dst, srcn in [("itmk", "tmk"), ("itmv", "tmv"), ("itmr", "tmr"),
                          ("icmk", "cmk"), ("icmr", "cmr")]:
            e.tensor_scalar(out=vcol(dst), in0=vcol(srcn), scalar1=-1.0,
                            scalar2=1.0, op0=OP.mult, op1=OP.add)
        return e.tensor_scalar(out=vcol("nw"), in0=vcol("w"), scalar1=-1.0,
                               scalar2=None, op0=OP.mult)
    g.step("cinv", "ve", ["consts", "cw"], c_inv)

    def emit_halo(tag, extra_deps):
        sc = lambda i: small[:, i:i + 1]
        ap = bass.AP(xf.tensor, 0, [[XW, 128], [128 * XW, 16]])
        g.step(f"hx{tag}", "sy", extra_deps,
               lambda e: e.dma_start(out=small[:, 0:16], in_=ap,
                                     allow_slow_non_contiguous=True), kind="d",
               dcls="hx" + tag)

        def a1(e):
            e.activation(out=smbf[:, 0:16], in_=small[:, 0:16], func=AF.Copy)
            return e.activation(out=smbf[:, 16:32], in_=small[:, 0:16], func=AF.Square)
        g.step(f"ha1{tag}", "ac", [f"hx{tag}", "cw"], a1)

        def p1(e):
            e.matmul(psb[6][:, 0:16], ones_bf[:, :], smbf[:, 0:16], start=True, stop=True)
            return e.matmul(psb[7][:, 0:16], ones_bf[:, :], smbf[:, 16:32],
                            start=True, stop=True)
        g.step(f"hp{tag}", "pe", [f"ha1{tag}", "minit"], p1)

        def v1(e):
            e.tensor_reduce(out=sc(16), in_=psb[6][:, 0:16], axis=AXX, op=OP.add)
            e.tensor_reduce(out=sc(17), in_=psb[7][:, 0:16], axis=AXX, op=OP.add)
            e.drain()
            e.tensor_scalar(out=sc(16), in0=sc(16), scalar1=RC, scalar2=None, op0=OP.mult)
            e.tensor_scalar(out=sc(17), in0=sc(17), scalar1=RC, scalar2=None, op0=OP.mult)
            e.drain()
            e.tensor_tensor(out=sc(18), in0=sc(16), in1=sc(16), op=OP.mult)
            e.drain()
            return e.tensor_tensor(out=sc(18), in0=sc(17), in1=sc(18), op=OP.subtract)
        g.step(f"hv1{tag}", "ve", [f"hp{tag}", "cinv"], v1)

        g.step(f"ha2{tag}", "ac", [f"hv1{tag}"],
               lambda e: e.activation(out=sc(19), in_=sc(18), func=AF.Sqrt, bias=eps_col[:, 0:1]))

        def v2(e):
            e.reciprocal(out=sc(20), in_=sc(19))
            e.drain()
            e.tensor_tensor(out=sc(21), in0=sc(18), in1=sc(20), op=OP.mult)
            e.drain()
            return e.tensor_tensor(out=sc(21), in0=sc(21), in1=sc(20), op=OP.mult)
        g.step(f"hv2{tag}", "ve", [f"ha2{tag}"], v2)

        g.step(f"ha3{tag}", "ac", [f"hv2{tag}"],
               lambda e: e.activation(out=sc(22), in_=sc(21), func=AF.Sqrt, bias=eps_col[:, 0:1]))

        def v3(e):
            e.reciprocal(out=sc(23), in_=sc(22))
            e.drain()
            e.tensor_tensor(out=sc(23), in0=sc(23), in1=sc(20), op=OP.mult)
            e.drain()
            e.tensor_scalar(out=small[:, 32:48], in0=small[:, 0:16],
                            scalar1=sc(16), scalar2=None, op0=OP.subtract)
            e.drain()
            e.tensor_scalar(out=small[:, 32:48], in0=small[:, 32:48],
                            scalar1=sc(23), scalar2=None, op0=OP.mult)
            e.tensor_tensor(out=small[:, 32:48], in0=small[:, 32:48],
                            in1=vcol("ln1w"), op=OP.mult)
            e.tensor_tensor(out=small[:, 32:48], in0=small[:, 32:48],
                            in1=vcol("ln1b"), op=OP.add)
            return e.tensor_scalar(out=hlast[:, :], in0=small[:, 32:48],
                                   scalar1=halo_sb[:, 0:1], scalar2=None, op0=OP.mult)
        g.step(f"hl{tag}", "ve", [f"ha3{tag}"], v3)
        return f"hl{tag}"

    hA = emit_halo("A", [])

    def ATMP(i):
        return P(0 if i % 2 == 0 else 2, 342, 342)

    def APK(i):
        return P(0 if i % 2 == 0 else 2, 0, 342)

    def APV(i):
        return P(1, 0 if i % 2 == 0 else 342, 342)

    def AH(i):
        return h_t if i % 2 == 0 else h_u

    NMX = NT * NQ
    _amix_prev = [None]

    for tau in range(NT):
        off = TOFF[tau]
        for q in range(NQ):
            i = tau * NQ + q
            deps = []
            if i >= 3:
                deps.append(f"Asa{i - 3}")
            xdma_step(f"Asx{tau}_{q}", q, tau, i % 3, deps,
                      eng="go" if q % 2 else "sy")

        for q in range(NQ):
            i = tau * NQ + q

            def a(e, i=i, tau=tau, q=q):
                e.activation(out=sbf(2 * (i % 3))[:, 0:342],
                             in_=xrs(i % 3)[:, 1:343], func=AF.Copy)
                return e.activation(out=sbf(2 * (i % 3) + 1)[:, 0:342],
                                    in_=xrs(i % 3)[:, 1:343], func=AF.Square)
            deps = [f"Asx{tau}_{q}"]
            if i >= 3:
                deps.append(f"Asp{i - 3}")
            g.step(f"Asa{i}", "ac", deps, a)

        for q in range(NQ):
            i = tau * NQ + q

            def p(e, i=i, q=q):
                e.matmul(psb[6][:, 0:342], ones_bf[:, :], sbf(2 * (i % 3))[:, 0:342],
                         start=(q == 0), stop=(q == NQ - 1))
                return e.matmul(psb[7][:, 0:342], ones_bf[:, :],
                                sbf(2 * (i % 3) + 1)[:, 0:342],
                                start=(q == 0), stop=(q == NQ - 1))
            deps = [f"Asa{i}", "minit"]
            if i == 0:
                deps.append("hv1A")
            if q == 0 and tau >= 1:
                deps.append(f"Av{tau - 1}")
            g.step(f"Asp{i}", "pe", deps, p)

        def v(e, tau=tau, off=off):
            e.tensor_scalar(out=S(0, 0, 342), in0=psb[6][:, 0:342], scalar1=RC,
                            scalar2=None, op0=OP.mult)
            e.tensor_copy(out=mu_full[:, off:off + 342], in_=S(0, 0, 342))
            e.tensor_tensor(out=S(0, 342, 342), in0=S(0, 0, 342), in1=S(0, 0, 342),
                            op=OP.mult)
            return e.scalar_tensor_tensor(out=S(0, 684, 342), in0=psb[7][:, 0:342],
                                          scalar=RC, in1=S(0, 342, 342),
                                          op0=OP.mult, op1=OP.subtract)
        g.step(f"Av{tau}", "ve", [f"Asp{tau * NQ + NQ - 1}"], v)

        g.step(f"Aa{tau}", "ac", [f"Av{tau}"],
               lambda e: e.activation(out=S(1, 0, 342), in_=S(0, 684, 342),
                                      func=AF.Sqrt, bias=eps_col[:, 0:1]))

        def v2(e, tau=tau, off=off):
            e.reciprocal(out=S(1, 342, 342), in_=S(1, 0, 342))
            e.tensor_copy(out=r0_full[:, off:off + 342], in_=S(1, 342, 342))
            e.tensor_tensor(out=S(1, 684, 342), in0=S(0, 684, 342),
                            in1=S(1, 342, 342), op=OP.mult)
            return e.tensor_tensor(out=S(1, 684, 342), in0=S(1, 684, 342),
                                   in1=S(1, 342, 342), op=OP.mult)
        g.step(f"Av2{tau}", "ve", [f"Aa{tau}"], v2)

        g.step(f"Aa2{tau}", "ac", [f"Av2{tau}"],
               lambda e: e.activation(out=S(2, 0, 342), in_=S(1, 684, 342),
                                      func=AF.Sqrt, bias=eps_col[:, 0:1]))

        def v3(e, tau=tau, off=off):
            e.reciprocal(out=S(2, 342, 342), in_=S(2, 0, 342))
            return e.tensor_tensor(out=rq_full[:, off:off + 342], in0=S(2, 342, 342),
                                   in1=S(1, 342, 342), op=OP.mult)
        g.step(f"Av3{tau}", "ve", [f"Aa2{tau}"], v3)


        for q in range(NQ):
            i = tau * NQ + q
            deps = []
            if i >= 2:
                deps.append(f"Amxp{i - 2}")
            xdma_step(f"Amxd{i}", q, tau, 3 + (i % 2), deps,
                      eng="sy" if q % 2 else "go")

            def mixp(e, i=i, off=off, tau=tau):
                e.tensor_tensor(out=ATMP(i), in0=xrs(3 + i % 2)[:, 1:343],
                                in1=mu_full[:, off:off + 342], op=OP.subtract)
                return e.tensor_tensor(out=ATMP(i), in0=ATMP(i),
                                       in1=rq_full[:, off:off + 342], op=OP.mult)
            deps = [f"Amxd{i}", f"Av3{tau}"]
            if i == 0:
                deps.append(hA)
            if i >= 2:
                deps.append(f"Amxa{i - 2}")
            g.step(f"Amxp{i}", "ve", deps, mixp)

            def mixa(e, i=i, q=q):
                h = AH(i)
                e.activation(out=h[:, 0:1], in_=hlast[:, q:q + 1], func=AF.Copy)
                return e.activation(out=h[:, 1:343], in_=ATMP(i), func=AF.Identity,
                                    scale=vch("ln1w", q), bias=vch("ln1b", q))
            deps = [f"Amxp{i}"]
            if i >= 2:
                deps.append(f"Amx{i - 2}")
            g.step(f"Amxa{i}", "ac", deps, mixa)

            def mix(e, i=i, q=q, off=off):
                h = AH(i)
                e.tensor_tensor(out=P(1, 0, 342), in0=h[:, 1:343],
                                in1=h[:, 0:342], op=OP.subtract)
                e.scalar_tensor_tensor(out=big(BIG0, q, off, 342),
                                       in0=P(1, 0, 342), scalar=vch("tmk", q),
                                       in1=h[:, 0:342], op0=OP.mult, op1=OP.add)
                e.scalar_tensor_tensor(out=big(BIG1, q, off, 342),
                                       in0=P(1, 0, 342), scalar=vch("tmv", q),
                                       in1=h[:, 0:342], op0=OP.mult, op1=OP.add)
                return e.tensor_copy(out=hlast[:, q:q + 1], in_=h[:, 342:343])
            if i >= 1:
                g.step(f"Amx{i - 1}", "ve", [f"Amxa{i - 1}"], _amix_prev[0])
            _amix_prev[0] = mix
    g.step(f"Amx{NT * NQ - 1}", "ve", [f"Amxa{NT * NQ - 1}"], _amix_prev[0])
    MIXDONE = f"Amx{NT * NQ - 1}"

    for nm, bb in [("xk", BIG0), ("xv", BIG1)]:
        if nm in taps:
            for q in range(NQ):
                def tp(e, q=q, bb=bb, nm=nm):
                    e.tensor_copy(out=tapbuf[:, :], in_=big(bb, q))
                    return e.dma_start(out=taps[nm][q * 128:(q + 1) * 128, :], in_=tapbuf[:, :])
                g.step(f"tap_{nm}{q}", "go", [MIXDONE], tp, kind="d", dcls="tap")

    def EP(q):
        return P(q % 2)

    def MT(q):
        return P(1 - q % 2)

    def emit_scan_h1(q):
        def sc1(e, q=q):
            e.tensor_scalar(out=P(3), in0=kslot(q), scalar1=0.0,
                            scalar2=vch("w", q), op0=OP.mult, op1=OP.subtract)
            e.tensor_tensor_scan(out=s_pad[:, 1:TP + 1], data0=P(3),
                                 data1=kslot(q), initial=NEG,
                                 op0=OP.add, op1=OP.max)
            e.tensor_tensor(out=S(1), in0=s_pad[:, 0:TP], in1=s_pad[:, 1:TP + 1],
                            op=OP.subtract)
            return e.tensor_tensor(out=S(2), in0=kslot(q), in1=s_pad[:, 1:TP + 1],
                                   op=OP.subtract)
        deps = [f"Aek{q}"]
        if q >= 1:
            deps.append(f"Asc2b{q - 1}")
        g.step(f"Asc1_{q}", "ve", deps, sc1)

        def sce(e, q=q):
            e.activation(out=S(1), in_=S(1), func=AF.Exp, bias=vch("nw", q))
            e.activation(out=S(3), in_=S(2), func=AF.Exp)
            return e.activation(out=EP(q), in_=kslot(q), func=AF.Identity,
                                bias=vch("tf", q))
        deps = [f"Asc1_{q}"]
        if q >= 1:
            deps.append(f"Asc2b{q - 1}")
        g.step(f"Asce{q}", "ac", deps, sce)

    def emit_scan_h2a(q):
        def sc2(e, q=q):
            e.tensor_tensor(out=S(4), in0=S(3), in1=vslot(q), op=OP.mult)
            e.tensor_tensor_scan(out=a_pad[:, 1:TP + 1], data0=S(1), data1=S(4),
                                 initial=0.0, op0=OP.mult, op1=OP.add)
            e.tensor_tensor_scan(out=b_pad[:, 1:TP + 1], data0=S(1), data1=S(3),
                                 initial=0.0, op0=OP.mult, op1=OP.add)
            return e.tensor_tensor(out=MT(q), in0=s_pad[:, 0:TP], in1=EP(q),
                                   op=OP.max)
        deps = [f"Asce{q}", f"Aev{q}"]
        if q >= 1:
            deps.append(f"Asc{q - 1}")
        g.step(f"Asc2_{q}", "ve", deps, sc2)

        def sc2b(e, q=q):
            e.tensor_copy(out=ndst[:, 0:TP], in_=MT(q))
            e.tensor_tensor(out=S(0), in0=s_pad[:, 0:TP], in1=ndst[:, 0:TP],
                            op=OP.subtract)
            e.tensor_tensor(out=EP(q), in0=EP(q), in1=ndst[:, 0:TP],
                            op=OP.subtract)
            return e.tensor_copy(out=summ_sb[:, q * 3:q * 3 + 1],
                                 in_=s_pad[:, 1024:1025])
        deps = [f"Asc2_{q}"]
        if q >= 1:
            deps.append(f"Aspl{q - 1}")
        g.step(f"Asc2b{q}", "ve", deps, sc2b)

    def emit_scan_h2b(q):
        def sce2(e, q=q):
            e.activation(out=S(0), in_=S(0), func=AF.Exp)
            return e.activation(out=EP(q), in_=EP(q), func=AF.Exp)
        g.step(f"Asce2_{q}", "ac", [f"Asc2b{q}"], sce2)

        def sc4(e, q=q):
            e.tensor_tensor(out=S(4), in0=vslot(q), in1=EP(q), op=OP.mult)
            e.tensor_tensor(out=P(2), in0=a_pad[:, 0:TP], in1=S(0), op=OP.mult)
            e.tensor_tensor(out=ndst[:, TP:2 * TP], in0=P(2), in1=S(4), op=OP.add)
            e.tensor_tensor(out=P(2), in0=b_pad[:, 0:TP], in1=S(0), op=OP.mult)
            e.tensor_tensor(out=ndst[:, 2 * TP:3 * TP], in0=P(2), in1=EP(q),
                            op=OP.add)
            e.tensor_copy(out=summ_sb[:, q * 3 + 1:q * 3 + 2],
                          in_=a_pad[:, 1024:1025])
            return e.tensor_copy(out=summ_sb[:, q * 3 + 2:q * 3 + 3],
                                 in_=b_pad[:, 1024:1025])
        g.step(f"Asc{q}", "ve", [f"Asce2_{q}"], sc4)

        def spl(e, q=q):
            ap = bass.AP(sp_nd.ap().tensor, q * 3 * 128 * TP,
                         [[TP, 128], [128 * TP, 3], [1, TP]])
            return e.dma_start(out=ap,
                               in_=ndst[:, :].rearrange("p (j t) -> p j t", j=3))
        g.step(f"Aspl{q}", "sy", [f"Asc{q}"], spl, kind="d", dcls=f"spnd{q % 2}")

        for nm, slot in [("k", kslot), ("v", vslot)]:
            if nm in taps:
                def tp(e, q=q, slot=slot, nm=nm):
                    return e.dma_start(out=taps[nm][q * 128:(q + 1) * 128, :],
                                       in_=slot(q))
                g.step(f"tap_{nm}{q}", "go", [f"Asc{q}"], tp, kind="d", dcls="tap")

    ensure_pieces(3)
    for mq in range(NQ):
        ik, iv = 2 * mq, 2 * mq + 1

        for tau in range(NT):
            def pk(e, ik=ik, tau=tau):
                off = TOFF[tau]
                for kq in range(NQ):
                    last = e.matmul(psb[tau][:, 0:342], wsl(ik, kq),
                                    big(BIG0, kq, off, 342),
                                    start=(kq == 0), stop=(kq == NQ - 1))
                return last
            deps = [cvn(ik), f"Amx{tau * NQ + NQ - 1}"]
            if mq >= 1:
                deps.append(f"Aek{mq - 1}")
            g.step(f"Apk{mq}t{tau}", "pe", deps, pk)

            def pv(e, iv=iv, tau=tau):
                off = TOFF[tau]
                for kq in range(NQ):
                    last = e.matmul(psb[3 + tau][:, 0:342], wsl(iv, kq),
                                    big(BIG1, kq, off, 342),
                                    start=(kq == 0), stop=(kq == NQ - 1))
                return last
            deps = [cvn(iv), f"Amx{tau * NQ + NQ - 1}"]
            if mq >= 1:
                deps.append(f"Aev{mq - 1}")
            g.step(f"Apv{mq}t{tau}", "pe", deps, pv)
        use_step[ik] = f"Apk{mq}t{NT - 1}"
        use_step[iv] = f"Apv{mq}t{NT - 1}"

        if mq >= 2:
            emit_scan_h2a(mq - 2)
        if mq >= 1:
            emit_scan_h1(mq - 1)
        if mq >= 2:
            emit_scan_h2b(mq - 2)

        def ek(e, mq=mq):
            for tau in range(NT):
                last = e.activation(out=kslot(mq, TOFF[tau], 342),
                                    in_=psb[tau][:, 0:342], func=AF.Copy)
            return last
        deps = [f"Apk{mq}t{NT - 1}"]
        if mq >= 2:
            deps.append(f"Asce{mq - 2}")
        g.step(f"Aek{mq}", "ac", deps, ek)

        def ev(e, mq=mq):
            for tau in range(NT):
                last = e.activation(out=vslot(mq, TOFF[tau], 342),
                                    in_=psb[3 + tau][:, 0:342], func=AF.Copy)
            return last
        deps = [f"Apv{mq}t{NT - 1}"]
        if mq >= 2:
            deps.append(f"Asc{mq - 2}")
        g.step(f"Aev{mq}", "ac", deps, ev)

        ensure_pieces(min(2 * (mq + 2) + 1, 31))
    emit_scan_h2a(NQ - 2)
    emit_scan_h1(NQ - 1)
    emit_scan_h2b(NQ - 2)
    emit_scan_h2a(NQ - 1)
    emit_scan_h2b(NQ - 1)
    ensure_pieces(34)
    hB = emit_halo("B", [f"Av{NT - 1}", f"Amxa{NT * NQ - 1}"])

    def summd(e):
        return e.dma_start(out=cc_in.ap(), in_=summ_sb[:, :])
    g.step("summd", "sy", [f"Asc{NQ - 1}"], summd, kind="d", dcls="ccin")

    if "summ" in taps:
        def tpsm(e):
            return e.dma_start(out=taps["summ"][:, :], in_=summ_sb[:, :])
        g.step("tap_summ", "go", [f"Asc{NQ - 1}"], tpsm, kind="d", dcls="tap")

    def coll(e):
        return e.collective_compute(
            "AllGather", OP.bypass, replica_groups=[list(range(NCORES))],
            ins=[cc_in.ap().opt()], outs=[cc_out.ap().opt()])
    g.step("cc", "go", ["summd"], coll, kind="c")


    for tau in range(NT):
        for q in range(NQ):
            i = tau * NQ + q
            deps = [f"Apv{NQ - 1}t{NT - 1}"]
            if i >= 5:
                deps.append(f"Bxrp{i - 5}")
            xdma_step(f"Bxd{i}", q, tau, i, deps)

    def BTMP(i):
        return P(0, 342 if i % 2 == 0 else 684, 342)

    def BPK(i):
        return P(0 if i % 2 == 0 else 1, 0, 342)

    for i in range(NMX):
        tau, q = divmod(i, NQ)
        off = TOFF[tau]

        def mixrp(e, i=i, off=off, tau=tau):
            e.tensor_tensor(out=BTMP(i), in0=xrs(i)[:, 1:343],
                            in1=mu_full[:, off:off + 342], op=OP.subtract)
            return e.tensor_tensor(out=BTMP(i), in0=BTMP(i),
                                   in1=rq_full[:, off:off + 342], op=OP.mult)
        deps = [f"Bxd{i}", f"Apk{NQ - 1}t{NT - 1}"]
        if i == 0:
            deps.append(hB)
        if i >= 2:
            deps.append(f"Bxra{i - 2}")
        g.step(f"Bxrp{i}", "ve", deps, mixrp)

        def mixra(e, i=i, q=q):
            h = AH(i)
            e.activation(out=h[:, 0:1], in_=hlast[:, q:q + 1], func=AF.Copy)
            e.activation(out=h[:, 1:343], in_=BTMP(i), func=AF.Identity,
                         scale=vch("ln1w", q), bias=vch("ln1b", q))
            e.activation(out=BPK(i), in_=h[:, 1:343], func=AF.Copy,
                         scale=vch("tmr", q))
            return e.activation(out=hlast[:, q:q + 1], in_=BTMP(i)[:, 341:342],
                                func=AF.Identity, scale=vch("ln1w", q),
                                bias=vch("ln1b", q))
        deps = [f"Bxrp{i}"]
        if i >= 2:
            deps.append(f"Bxr{i - 2}")
        g.step(f"Bxra{i}", "ac", deps, mixra)

        def mixr(e, i=i, q=q, off=off):
            return e.scalar_tensor_tensor(out=big(BIG0, q, off, 342),
                                          in0=AH(i)[:, 0:342],
                                          scalar=vch("itmr", q), in1=BPK(i),
                                          op0=OP.mult, op1=OP.add)
        if i >= 1:
            g.step(f"Bxr{i - 1}", "ve", [f"Bxra{i - 1}"], mixr_prev)
        mixr_prev = mixr
    g.step(f"Bxr{NMX - 1}", "ve", [f"Bxra{NMX - 1}"], mixr_prev)
    XRDONE = f"Bxr{NT * NQ - 1}"

    for q in range(NQ):
        def gad(e, q=q):
            ap = bass.AP(cc_out.ap().tensor, q * 3,
                         [[48, 128], [128 * 48, 8], [1, 3]])
            return e.dma_start(out=P(2, q * 24, 24).rearrange("p (i j) -> p i j", j=3),
                               in_=ap)
        g.step(f"gad{q}", "sy", ["cc", f"Asc{NQ - 1}"], gad, kind="d", dcls="gad")

    def g_s(q):
        return P(2, 640 + q * 8, 8)

    def g_a(q):
        return P(2, 768 + q * 8, 8)

    def g_b(q):
        return P(2, 384 + q * 8, 8)

    def g_t(q):
        return P(2, 512 + q * 8, 8)

    def comb0(e):
        for q in range(NQ):
            v = P(2, q * 24, 24).rearrange("p (i j) -> p j i", j=3)
            e.tensor_copy(out=g_s(q), in_=v[:, 0, :])
            e.tensor_copy(out=g_a(q), in_=v[:, 1, :])
            last = e.tensor_copy(out=g_b(q), in_=v[:, 2, :])
        return last
    g.step("comb0", "ve", [f"gad{q}" for q in range(NQ)], comb0)

    def comb1(e):
        for q in range(NQ):
            e.tensor_scalar(out=g_t(q), in0=shifts_sb[:, :],
                            scalar1=vch("w", q), scalar2=None, op0=OP.mult)
        for q in range(NQ):
            e.tensor_tensor(out=g_t(q), in0=g_s(q), in1=g_t(q), op=OP.subtract)
        for q in range(NQ):
            e.tensor_reduce(out=carry(q, 2), in_=g_t(q), axis=AXX, op=OP.max)
        for q in range(NQ):
            e.tensor_scalar(out=g_t(q), in0=g_t(q),
                            scalar1=carry(q, 2), scalar2=None, op0=OP.subtract)
        return e.engine_nop()
    g.step("comb1", "ve", ["comb0"], comb1)

    def combe(e):
        for q in range(NQ):
            last = e.activation(out=g_t(q), in_=g_t(q), func=AF.Exp)
        return last
    g.step("combe", "ac", ["comb1"], combe)

    def comb2(e):
        for q in range(NQ):
            e.tensor_tensor(out=g_t(q), in0=g_t(q), in1=masks_sb[:, :], op=OP.mult)
        for q in range(NQ):
            e.tensor_tensor(out=g_a(q), in0=g_a(q), in1=g_t(q), op=OP.mult)
        for q in range(NQ):
            e.tensor_reduce(out=carry(q, 0), in_=g_a(q), axis=AXX, op=OP.add)
        for q in range(NQ):
            e.tensor_tensor(out=g_b(q), in0=g_b(q), in1=g_t(q), op=OP.mult)
        for q in range(NQ):
            last = e.tensor_reduce(out=carry(q, 1), in_=g_b(q), axis=AXX, op=OP.add)
        return last
    g.step("comb2", "ve", ["combe"], comb2)

    if "carry" in taps:
        def tpc(e):
            return e.dma_start(out=taps["carry"][:, :], in_=carry_sb[:, 0:48])
        g.step("tap_carry", "go", ["comb2"], tpc, kind="d", dcls="tap")


    g.step("Biota", "go", [],
           lambda e: e.iota(cring[:, 0:TP], [[1, TP]], channel_multiplier=0,
                            allow_small_or_imprecise_dtypes=True))

    def emit_fixup(q):
        def ndb(e, q=q):
            ap = bass.AP(sp_nd.ap().tensor, q * 3 * 128 * TP,
                         [[TP, 128], [128 * TP, 3], [1, TP]])
            b0 = (q % 2) * 3 * TP
            return e.dma_start(out=kv_bf[:, b0:b0 + 3 * TP].rearrange(
                "p (j t) -> p j t", j=3), in_=ap)
        deps = [f"Aspl{q}", f"Asc{NQ - 1}"]
        if q >= 2:
            deps.append(f"Bfx{q - 2}")
        g.step(f"Bndb{q}", "sy", deps, ndb, kind="d", dcls=f"nb{q % 2}")

        def fx1(e, q=q):
            e.tensor_scalar(out=S(0), in0=cring[:, 0:TP], scalar1=vch("nw", q),
                            scalar2=carry(q, 2), op0=OP.mult, op1=OP.add)
            e.tensor_tensor(out=S(1), in0=ndm(q, 0), in1=S(0), op=OP.max)
            e.tensor_tensor(out=S(2), in0=ndm(q, 0), in1=S(1), op=OP.subtract)
            return e.tensor_tensor(out=S(0), in0=S(0), in1=S(1), op=OP.subtract)
        deps = [f"Bndb{q}", "Biota", XRDONE, "comb2"]
        g.step(f"Bfx1_{q}", "ve", deps, fx1)

        def fxe(e, q=q):
            e.activation(out=S(2), in_=S(2), func=AF.Exp)
            return e.activation(out=S(0), in_=S(0), func=AF.Exp)
        g.step(f"Bfxe{q}", "ac", [f"Bfx1_{q}"], fxe)

        def fxg(e, q=q):
            e.tensor_tensor(out=S(3), in0=ndm(q, 1), in1=S(2), op=OP.mult)
            return e.tensor_tensor(out=S(4), in0=ndm(q, 2), in1=S(2), op=OP.mult)
        g.step(f"Bfxg{q}", "ve", [f"Bfxe{q}"], fxg)

        def fx2(e, q=q):
            e.scalar_tensor_tensor(out=S(3), in0=S(0), scalar=carry(q, 0),
                                   in1=S(3), op0=OP.mult, op1=OP.add)
            e.scalar_tensor_tensor(out=S(4), in0=S(0), scalar=carry(q, 1),
                                   in1=S(4), op0=OP.mult, op1=OP.add)
            e.reciprocal(out=S(5), in_=S(4))
            e.tensor_tensor(out=S(3), in0=S(3), in1=S(5), op=OP.mult)
            return e.tensor_tensor(out=big(BIG1, q), in0=S(3), in1=srslot(q),
                                   op=OP.mult)
        g.step(f"Bfx{q}", "ve", [f"Bfxg{q}", f"Besr{q}", f"Apv{NQ - 1}t{NT - 1}"], fx2)

        for nm in ("sr", "rwkv"):
            if nm in taps:
                def tp(e, q=q, nm=nm):
                    if nm == "sr":
                        e.tensor_copy(out=tapbuf[:, :], in_=srslot(q))
                    else:
                        e.tensor_copy(out=tapbuf[:, :], in_=big(BIG1, q))
                    return e.dma_start(out=taps[nm][q * 128:(q + 1) * 128, :], in_=tapbuf[:, :])
                g.step(f"tap_{nm}{q}", "go", [f"Bfx{q}"], tp, kind="d", dcls="tap")

    ensure_pieces(34)
    for mq in range(NQ):
        i = 32 + mq

        for tau in range(NT):
            def pr(e, i=i, tau=tau):
                off = TOFF[tau]
                for kq in range(NQ):
                    last = e.matmul(psb[tau][:, 0:342], wsl(i, kq),
                                    big(BIG0, kq, off, 342),
                                    start=(kq == 0), stop=(kq == NQ - 1))
                return last
            deps = [cvn(i), f"Bxr{tau * NQ + NQ - 1}"]
            if mq >= 1:
                deps.append(f"Besr{mq - 1}")
            g.step(f"Bpr{mq}t{tau}", "pe", deps, pr)
        use_step[i] = f"Bpr{mq}t{NT - 1}"

        def esr(e, mq=mq):
            for tau in range(NT):
                last = e.activation(out=srslot(mq, TOFF[tau], 342),
                                    in_=psb[tau][:, 0:342], func=AF.Sigmoid)
            return last
        deps = [f"Bpr{mq}t{NT - 1}"]
        if mq >= 3:
            deps.append(f"Bfx{mq - 3}")
        g.step(f"Besr{mq}", "ac", deps, esr)

        if mq >= 1:
            emit_fixup(mq - 1)
        ensure_pieces(32 + mq + 2)
    emit_fixup(NQ - 1)


    ensure_pieces(49)
    for mq in range(NQ):
        i = 48 + mq
        for tau in range(NT):
            i2 = mq * NT + tau
            deps = []
            if i2 >= 3:
                deps.append(f"Bx1_{i2 - 3}")
            else:
                deps.append(f"Bxr{45 + i2}")
            xdma_step(f"Bx2d{i2}", mq, tau, i2, deps)

        for tau in range(NT):
            i2 = mq * NT + tau

            def po(e, i=i, mq=mq, tau=tau, i2=i2):
                off = TOFF[tau]
                for kq in range(NQ):
                    last = e.matmul(psb[i2 % 2][:, 0:342], wsl(i, kq),
                                    big(BIG1, kq, off, 342),
                                    start=(kq == 0), stop=(kq == NQ - 1))
                return last
            deps = [cvn(i), f"Bfx{NQ - 1}"]
            if i2 >= 2:
                deps.append(f"Bx1_{i2 - 2}")
            g.step(f"Bpo{mq}t{tau}", "pe", deps, po)
        use_step[i] = f"Bpo{mq}t{NT - 1}"

        for tau in range(NT):
            i2 = mq * NT + tau

            def x1f(e, mq=mq, tau=tau, i2=i2):
                off = TOFF[tau]
                e.tensor_tensor(out=S(0, 0, 342), in0=xrs(i2)[:, 1:343],
                                in1=mu_full[:, off:off + 342], op=OP.subtract)
                e.tensor_tensor(out=S(0, 0, 342), in0=S(0, 0, 342),
                                in1=r0_full[:, off:off + 342], op=OP.mult)
                return e.tensor_tensor(out=sbf(i2 % 3)[:, 0:342], in0=S(0, 0, 342),
                                       in1=psb[i2 % 2][:, 0:342], op=OP.add)
            deps = [f"Bpo{mq}t{tau}", f"Bx2d{i2}", XRDONE]
            if i2 >= 3:
                deps.append(f"Bspl{i2 - 3}")
            g.step(f"Bx1_{i2}", "ve", deps, x1f)

            def sq(e, i2=i2):
                return e.activation(out=sbf(3 + (i2 % 3))[:, 0:342],
                                    in_=sbf(i2 % 3)[:, 0:342], func=AF.Square)
            deps = [f"Bx1_{i2}"]
            if i2 >= 3:
                deps.append(f"Bst{i2 - 3}")
            g.step(f"Bsq{i2}", "ac", deps, sq)

            def stat(e, mq=mq, tau=tau, i2=i2):
                e.matmul(psb[2 + tau][:, 0:342], ones_bf[:, :], sbf(i2 % 3)[:, 0:342],
                         start=(mq == 0), stop=(mq == NQ - 1))
                return e.matmul(psb[5 + tau][:, 0:342], ones_bf[:, :],
                                sbf(3 + (i2 % 3))[:, 0:342],
                                start=(mq == 0), stop=(mq == NQ - 1))
            g.step(f"Bst{i2}", "pe", [f"Bsq{i2}"], stat)

            def spl(e, mq=mq, tau=tau, i2=i2):
                ap = bass.AP(sp_x1.ap().tensor, mq * 128 * TP + TOFF[tau],
                             [[TP, 128], [1, 342]])
                return e.dma_start(out=ap, in_=sbf(i2 % 3)[:, 0:342])
            g.step(f"Bspl{i2}", "sy", [f"Bst{i2}"], spl, kind="d", dcls=f"sp{i2 % 3}")

            if "x1" in taps and tau == NT - 1:
                def tp(e, mq=mq):
                    ap = bass.AP(sp_x1.ap().tensor, mq * 128 * TP, [[TP, 128], [1, TP]])
                    return e.dma_start(out=taps["x1"][mq * 128:(mq + 1) * 128, :], in_=ap)
                g.step(f"tap_x1{i2}", "go",
                       [f"Bspl{mq * NT}", f"Bspl{mq * NT + 1}", f"Bspl{mq * NT + 2}"],
                       tp, kind="d", dcls="tap")
        if True:
            ensure_pieces(48 + mq + 2)

    for tau in range(NT):
        off = TOFF[tau]

        def ln2v(e, tau=tau, off=off):
            e.tensor_scalar(out=P(0, 0, 342), in0=psb[2 + tau][:, 0:342],
                            scalar1=RC, scalar2=None, op0=OP.mult)
            e.tensor_copy(out=mu_full[:, off:off + 342], in_=P(0, 0, 342))
            e.tensor_tensor(out=P(0, 342, 342), in0=P(0, 0, 342), in1=P(0, 0, 342),
                            op=OP.mult)
            return e.scalar_tensor_tensor(out=P(0, 684, 342), in0=psb[5 + tau][:, 0:342],
                                          scalar=RC, in1=P(0, 342, 342),
                                          op0=OP.mult, op1=OP.subtract)
        g.step(f"Bln2v{tau}", "ve", [f"Bst{(NQ - 1) * NT + NT - 1}"], ln2v)

        g.step(f"Bln2a{tau}", "ac", [f"Bln2v{tau}"],
               lambda e: e.activation(out=P(1, 0, 342), in_=P(0, 684, 342),
                                      func=AF.Sqrt, bias=eps_col[:, 0:1]))

        def ln2r(e, tau=tau, off=off):
            e.reciprocal(out=P(1, 342, 342), in_=P(1, 0, 342))
            return e.tensor_copy(out=r0_full[:, off:off + 342], in_=P(1, 342, 342))
        g.step(f"Bln2r{tau}", "ve", [f"Bln2a{tau}"], ln2r)
    LN2 = f"Bln2r{NT - 1}"


    def GH(q, off=0, n=TP + 1):
        b = 0 if q % 2 == 0 else 4 * TP
        return BIG2f[:, b + off:b + off + n]

    def GPR(q):
        b = TP + 1 if q % 2 == 0 else 5 * TP + 1
        return BIG2f[:, b:b + TP]

    def GPK(q):
        b = 2 * TP if q % 2 == 0 else 6 * TP + 1
        return BIG2f[:, b:b + TP]

    for q in range(NQ):
        def x1b(e, q=q):
            ap = bass.AP(sp_x1.ap().tensor, q * 128 * TP, [[TP, 128], [1, TP]])
            return e.dma_start(out=x1back(q), in_=ap)
        deps = [f"Bspl{q * NT}", f"Bspl{q * NT + 1}", f"Bspl{q * NT + 2}",
                "comb2"]
        if q >= 2:
            deps.append(f"Cgp{q - 2}")
        g.step(f"Cx1b{q}", "sy", deps, x1b, kind="d", dcls=f"xb{q % 2}")

        def gp(e, q=q):
            e.memset(GH(q, 0, 1), 0.0)
            e.tensor_tensor(out=GPK(q), in0=x1back(q), in1=mu_full[:, :],
                            op=OP.subtract)
            return e.tensor_tensor(out=GPK(q), in0=GPK(q), in1=r0_full[:, :],
                                   op=OP.mult)
        deps = [f"Cx1b{q}", LN2, f"Bfx{NQ - 1}"]
        if q >= 2:
            deps.append(f"Cg{q - 2}")
        g.step(f"Cgp{q}", "ve", deps, gp)

        def ga(e, q=q):
            e.activation(out=GH(q, 1, TP), in_=GPK(q), func=AF.Identity,
                         scale=vch("ln2w", q), bias=vch("ln2b", q))
            e.activation(out=GPK(q), in_=GH(q, 1, TP), func=AF.Copy,
                         scale=vch("cmk", q))
            return e.activation(out=GPR(q), in_=GH(q, 1, TP), func=AF.Copy,
                                scale=vch("cmr", q))
        deps = [f"Cgp{q}"]
        if q >= 2:
            deps.append(f"Cg{q - 2}")
        g.step(f"Cga{q}", "ac", deps, ga)

        def gf(e, q=q):
            e.scalar_tensor_tensor(out=big(BIG0, q), in0=GH(q, 0, TP),
                                   scalar=vch("icmk", q), in1=GPK(q),
                                   op0=OP.mult, op1=OP.add)
            return e.scalar_tensor_tensor(out=big(BIG1, q), in0=GH(q, 0, TP),
                                          scalar=vch("icmr", q), in1=GPR(q),
                                          op0=OP.mult, op1=OP.add)
        if q >= 1:
            g.step(f"Cg{q - 1}", "ve",
                   [f"Cga{q - 1}", f"Bpr{NQ - 1}t{NT - 1}",
                    f"Bpo{NQ - 1}t{NT - 1}"], gf_prev)
        gf_prev = gf
    g.step(f"Cg{NQ - 1}", "ve",
           [f"Cga{NQ - 1}", f"Bpr{NQ - 1}t{NT - 1}",
            f"Bpo{NQ - 1}t{NT - 1}"], gf_prev)
    GDONE = f"Cg{NQ - 1}"

    for nm, bb in [("xk2", BIG0), ("xr2", BIG1)]:
        if nm in taps:
            for q in range(NQ):
                def tp(e, q=q, bb=bb, nm=nm):
                    e.tensor_copy(out=tapbuf[:, :], in_=big(bb, q))
                    return e.dma_start(out=taps[nm][q * 128:(q + 1) * 128, :], in_=tapbuf[:, :])
                g.step(f"tap_{nm}{q}", "go", [GDONE], tp, kind="d", dcls="tap")

    ensure_pieces(65)
    for mq in range(NQ):
        i = 64 + mq

        def pcr(e, i=i, mq=mq):
            for tau in range(NT):
                off = TOFF[tau]
                for kq in range(NQ):
                    last = e.matmul(psb[tau][:, 0:342], wsl(i, kq),
                                    big(BIG1, kq, off, 342),
                                    start=(kq == 0), stop=(kq == NQ - 1))
            return last
        deps = [cvn(i), GDONE]
        if mq >= 1:
            deps.append(f"Cescr{mq - 1}")
        g.step(f"Cpcr{mq}", "pe", deps, pcr)
        use_step[i] = f"Cpcr{mq}"

        def escr(e, mq=mq):
            for tau in range(NT):
                last = e.activation(out=scrback(mq, TOFF[tau], 342),
                                    in_=psb[tau][:, 0:342], func=AF.Sigmoid)
            return last
        deps = [f"Cpcr{mq}"]
        if mq >= 2:
            deps.append(f"Cscspl{mq - 2}")
        g.step(f"Cescr{mq}", "ac", deps, escr)

        def scspl(e, mq=mq):
            ap = bass.AP(sp_scr.ap().tensor, mq * 128 * TP, [[TP, 128], [1, TP]])
            return e.dma_start(out=ap, in_=scrback(mq))
        g.step(f"Cscspl{mq}", "sy", [f"Cescr{mq}"], scspl, kind="d", dcls=f"ss{mq % 2}")

        if "scr" in taps:
            def tp(e, mq=mq):
                ap = bass.AP(sp_scr.ap().tensor, mq * 128 * TP, [[TP, 128], [1, TP]])
                return e.dma_start(out=taps["scr"][mq * 128:(mq + 1) * 128, :], in_=ap)
            g.step(f"tap_scr{mq}", "go", [f"Cscspl{mq}"], tp, kind="d", dcls="tap")
        ensure_pieces(64 + mq + 2)

    for fb in range(4):
        base = 80 + fb * 32
        for m2 in range(16):
            i = base + m2

            for tau in range(NT):
                i3 = m2 * NT + tau

                def pck(e, i=i, m2=m2, tau=tau, i3=i3):
                    off = TOFF[tau]
                    for kq in range(NQ):
                        last = e.matmul(psb[i3 % 2][:, 0:342], wsl(i, kq),
                                        big(BIG0, kq, off, 342),
                                        start=(kq == 0), stop=(kq == NQ - 1))
                    return last
                deps = [cvn(i), GDONE, f"Cescr{NQ - 1}"]
                if i3 >= 2:
                    deps.append(f"Ckf{fb}_{i3 - 2}")
                elif fb >= 1:
                    deps.append(f"Ckf{fb - 1}_{15 * NT + NT - 1}")
                g.step(f"Cpck{fb}_{m2}t{tau}", "pe", deps, pck)

                def kfe(e, m2=m2, tau=tau, i3=i3, fb=fb):
                    b = i3 % 2
                    e.activation(out=P(0, 0, 342), in_=psb[b][:, 0:342], func=AF.Relu)
                    return e.activation(out=big(BIG2, m2, TOFF[tau], 342),
                                        in_=P(0, 0, 342), func=AF.Square)
                deps = [f"Cpck{fb}_{m2}t{tau}"]
                if fb >= 1:
                    deps.append(f"Ckv{fb - 1}_{NQ - 1}_{NT - 1}")
                g.step(f"Ckf{fb}_{i3}", "ac", deps, kfe)
            use_step[i] = f"Cpck{fb}_{m2}t{NT - 1}"
            ensure_pieces(i + 2)

        for mq in range(16):
            i = base + 16 + mq

            for tau in range(NT):
                i2 = mq * NT + tau

                def pcv(e, i=i, mq=mq, fb=fb, tau=tau, i2=i2):
                    off = TOFF[tau]
                    for kq in range(NQ):
                        last = e.matmul(psb[2 + i2 % 2][:, 0:342], wsl(i, kq),
                                        big(BIG2, kq, off, 342),
                                        start=(kq == 0), stop=(kq == NQ - 1))
                    return last
                deps = [cvn(i), f"Ckf{fb}_{15 * NT + NT - 1}"]
                if i2 >= 2:
                    deps.append(f"Ckv{fb}_{(i2 - 2) // NT}_{(i2 - 2) % NT}")
                g.step(f"Cpcv{fb}_{mq}t{tau}", "pe", deps, pcv)

                def kva(e, mq=mq, tau=tau, fb=fb, i2=i2):
                    b = 2 + i2 % 2
                    s_ = mq * NT + tau
                    if fb == 0:
                        return e.tensor_copy(out=kvsl(s_), in_=psb[b][:, 0:342])
                    return e.tensor_tensor(out=kvsl(s_), in0=kvsl(s_),
                                           in1=psb[b][:, 0:342], op=OP.add)
                deps = [f"Cpcv{fb}_{mq}t{tau}"]
                if fb == 0:
                    deps += [f"Bpo{NQ - 1}t{NT - 1}", GDONE]
                g.step(f"Ckv{fb}_{mq}_{tau}", "ve", deps, kva)
            use_step[i] = f"Cpcv{fb}_{mq}t{NT - 1}"
            ensure_pieces(i + 2)

    KVDONE = f"Ckv3_{NQ - 1}_{NT - 1}"

    if "kvt" in taps:
        for q in range(NQ):
            def tp(e, q=q):
                for tau in range(NT):
                    e.tensor_copy(out=tapbuf[:, TOFF[tau]:TOFF[tau]+342], in_=kvsl(q * NT + tau))
                return e.dma_start(out=taps["kvt"][q * 128:(q + 1) * 128, :], in_=tapbuf[:, :])
            g.step(f"tap_kvt{q}", "go", [KVDONE], tp, kind="d", dcls="tap")

    for mq in range(NQ):
        def x1b2(e, mq=mq):
            ap = bass.AP(sp_x1.ap().tensor, mq * 128 * TP, [[TP, 128], [1, TP]])
            return e.dma_start(out=x1back(mq), in_=ap)
        deps = [KVDONE]
        if mq >= 2:
            deps.append(f"Cx2_{mq - 2}_{NT - 1}")
        g.step(f"Cx1c{mq}", "sy", deps, x1b2, kind="d", dcls=f"xb{mq % 2}")

        def scrb(e, mq=mq):
            ap = bass.AP(sp_scr.ap().tensor, mq * 128 * TP, [[TP, 128], [1, TP]])
            return e.dma_start(out=scrback(mq), in_=ap)
        deps = [f"Cscspl{mq}", KVDONE]
        if mq >= 2:
            deps.append(f"Cx2_{mq - 2}_{NT - 1}")
        g.step(f"Cscrb{mq}", "sy", deps, scrb, kind="d", dcls=f"sb{mq % 2}")

        for tau in range(NT):
            def x2f(e, mq=mq, tau=tau):
                s_ = mq * NT + tau
                off = TOFF[tau]
                e.tensor_tensor(out=kvsl(s_), in0=kvsl(s_),
                                in1=scrback(mq, off, 342), op=OP.mult)
                return e.tensor_tensor(out=kvsl(s_), in0=kvsl(s_),
                                       in1=x1back(mq, off, 342), op=OP.add)
            g.step(f"Cx2_{mq}_{tau}", "ve", [f"Cx1c{mq}", f"Cscrb{mq}"], x2f)

            def x2t(e, mq=mq, tau=tau):
                s_ = mq * NT + tau
                e.activation(out=sbf(s_ % 3)[:, 0:342], in_=kvsl(s_), func=AF.Copy)
                return e.activation(out=sbf(3 + s_ % 3)[:, 0:342], in_=kvsl(s_),
                                    func=AF.Square)
            deps = [f"Cx2_{mq}_{tau}"]
            if mq * NT + tau >= 3:
                deps.append(f"Cst{mq * NT + tau - 3}")
            g.step(f"Cx2t{mq * NT + tau}", "ac", deps, x2t)

            def st(e, mq=mq, tau=tau):
                s_ = mq * NT + tau
                e.matmul(psb[2 + tau][:, 0:342], ones_bf[:, :], sbf(s_ % 3)[:, 0:342],
                         start=(mq == 0), stop=(mq == NQ - 1))
                return e.matmul(psb[5 + tau][:, 0:342], ones_bf[:, :],
                                sbf(3 + s_ % 3)[:, 0:342],
                                start=(mq == 0), stop=(mq == NQ - 1))
            deps = [f"Cx2t{mq * NT + tau}"]
            if mq == 0:
                deps.append(f"Cpcv3_{NQ - 1}t{NT - 1}")
            g.step(f"Cst{mq * NT + tau}", "pe", deps, st)

            if "x2" in taps:
                def tp(e, mq=mq, tau=tau):
                    s_ = mq * NT + tau
                    e.tensor_copy(out=tapbuf[:, 0:342], in_=kvsl(s_))
                    return e.dma_start(out=taps["x2"][mq * 128:(mq + 1) * 128,
                                                      TOFF[tau]:TOFF[tau] + 342],
                                       in_=tapbuf[:, 0:342])
                g.step(f"tap_x2_{mq}_{tau}", "go", [f"Cx2_{mq}_{tau}"], tp,
                       kind="d", dcls="tap")

    for tau in range(NT):
        def l3v(e, tau=tau):
            e.tensor_scalar(out=P(0, tau * 342, 342), in0=psb[2 + tau][:, 0:342],
                            scalar1=RC, scalar2=None, op0=OP.mult)
            e.tensor_tensor(out=P(2, 0, 342), in0=P(0, tau * 342, 342),
                            in1=P(0, tau * 342, 342), op=OP.mult)
            return e.scalar_tensor_tensor(out=P(2, 342, 342), in0=psb[5 + tau][:, 0:342],
                                          scalar=RC, in1=P(2, 0, 342),
                                          op0=OP.mult, op1=OP.subtract)
        g.step(f"Cl3v{tau}", "ve", [f"Cst{(NQ - 1) * NT + tau}"], l3v)

        g.step(f"Cl3a{tau}", "ac", [f"Cl3v{tau}"],
               lambda e: e.activation(out=P(2, 684, 342), in_=P(2, 342, 342),
                                      func=AF.Sqrt, bias=eps_col[:, 0:1]))

        def l3r(e, tau=tau):
            return e.reciprocal(out=P(1, tau * 342, 342), in_=P(2, 684, 342))
        g.step(f"Cl3r{tau}", "ve", [f"Cl3a{tau}"], l3r)
    LN3 = f"Cl3r{NT - 1}"

    for mq in range(NQ):
        for tau in range(NT):
            i4 = mq * NT + tau
            deps = [LN3]
            if i4 >= 3:
                deps.append(f"CX1_{i4 - 3}")
            xdma_step(f"Cxd{i4}", mq, tau, i4, deps)

            def X1f(e, mq=mq, tau=tau, i4=i4):
                s_ = mq * NT + tau
                e.tensor_tensor(out=S(2, 0, 342), in0=kvsl(s_),
                                in1=P(0, tau * 342, 342), op=OP.subtract)
                e.tensor_tensor(out=S(2, 0, 342), in0=S(2, 0, 342),
                                in1=P(1, tau * 342, 342), op=OP.mult)
                return e.tensor_tensor(out=S(2, 342, 342), in0=S(2, 0, 342),
                                       in1=xrs(i4)[:, 1:343], op=OP.add)
            g.step(f"CX1_{i4}", "ve", [f"Cxd{i4}", LN3], X1f)

            def X1s(e, mq=mq, tau=tau):
                ap = bass.AP(sp_X1.ap().tensor, (mq * 128 * TP + TOFF[tau]) if False
                             else mq * 128 * TP + TOFF[tau], [[TP, 128], [1, 342]])
                return e.dma_start(out=ap, in_=S(2, 342, 342))
            g.step(f"CX1s{i4}", "sy", [f"CX1_{i4}"], X1s, kind="d", dcls="spX")

            def X1t(e, i4=i4):
                e.activation(out=sbf(i4 % 3)[:, 0:342], in_=S(2, 342, 342), func=AF.Copy)
                return e.activation(out=sbf(3 + i4 % 3)[:, 0:342], in_=S(2, 342, 342),
                                    func=AF.Square)
            deps = [f"CX1_{i4}", f"CX1s{i4}"]
            if i4 >= 3:
                deps.append(f"Cst4_{i4 - 3}")
            g.step(f"CX1t{i4}", "ac", deps, X1t)

            def st4(e, mq=mq, tau=tau, i4=i4):
                e.matmul(psb[2 + tau][:, 0:342], ones_bf[:, :], sbf(i4 % 3)[:, 0:342],
                         start=(mq == 0), stop=(mq == NQ - 1))
                return e.matmul(psb[5 + tau][:, 0:342], ones_bf[:, :],
                                sbf(3 + i4 % 3)[:, 0:342],
                                start=(mq == 0), stop=(mq == NQ - 1))
            deps = [f"CX1t{i4}"]
            if mq == 0:
                deps.append(f"Cl3v{tau}")
            g.step(f"Cst4_{i4}", "pe", deps, st4)

    for snm, fn, deps, kind, _dc in g.steps["ve"]:
        if snm.startswith("CX1_"):
            i4 = int(snm[4:])
            if i4 >= 1 and f"CX1s{i4 - 1}" in g.ms:
                deps.append(f"CX1s{i4 - 1}")

    if "X1" in taps:
        for mq in range(NQ):
            def tp(e, mq=mq):
                ap = bass.AP(sp_X1.ap().tensor, mq * 128 * TP, [[TP, 128], [1, TP]])
                return e.dma_start(out=taps["X1"][mq * 128:(mq + 1) * 128, :], in_=ap)
            g.step(f"tap_X1{mq}", "go", [f"CX1s{mq * NT + NT - 1}", KVDONE], tp,
                   kind="d", dcls="tap")

    for tau in range(NT):
        def l4v(e, tau=tau):
            e.tensor_scalar(out=P(0, tau * 342, 342), in0=psb[2 + tau][:, 0:342],
                            scalar1=RC, scalar2=None, op0=OP.mult)
            e.tensor_tensor(out=P(2, 0, 342), in0=P(0, tau * 342, 342),
                            in1=P(0, tau * 342, 342), op=OP.mult)
            return e.scalar_tensor_tensor(out=P(2, 342, 342), in0=psb[5 + tau][:, 0:342],
                                          scalar=RC, in1=P(2, 0, 342),
                                          op0=OP.mult, op1=OP.subtract)
        g.step(f"Cl4v{tau}", "ve", [f"Cst4_{(NQ - 1) * NT + tau}"], l4v)

        g.step(f"Cl4a{tau}", "ac", [f"Cl4v{tau}"],
               lambda e: e.activation(out=P(2, 684, 342), in_=P(2, 342, 342),
                                      func=AF.Sqrt, bias=eps_col[:, 0:1]))

        def l4r(e, tau=tau):
            return e.reciprocal(out=P(1, tau * 342, 342), in_=P(2, 684, 342))
        g.step(f"Cl4r{tau}", "ve", [f"Cl4a{tau}"], l4r)
    LN4 = f"Cl4r{NT - 1}"

    def zring(q, off=0, n=TP):
        b0 = (q % 2) * TP
        return kv_sb[:, b0 + off:b0 + off + n]

    for mq in range(NQ):
        def X1b3(e, mq=mq):
            ap = bass.AP(sp_X1.ap().tensor, mq * 128 * TP, [[TP, 128], [1, TP]])
            return e.dma_start(out=zring(mq), in_=ap)
        deps = [LN4]
        if mq >= 2:
            deps.append(f"Cz{mq - 2}")
        g.step(f"CX1b{mq}", "sy", deps, X1b3, kind="d", dcls=f"z{mq % 2}")

        def zf(e, mq=mq):
            for tau in range(NT):
                off = TOFF[tau]
                e.tensor_tensor(out=S(2, 0, 342), in0=zring(mq, off, 342),
                                in1=P(0, tau * 342, 342), op=OP.subtract)
                last = e.tensor_tensor(out=big(BIG0, mq, off, 342), in0=S(2, 0, 342),
                                       in1=P(1, tau * 342, 342), op=OP.mult)
            return last
        g.step(f"Cz{mq}", "ve", [f"CX1b{mq}", LN4, f"Cpck3_15t{NT - 1}"], zf)
    ZDONE = f"Cz{NQ - 1}"

    if "z" in taps:
        for q in range(NQ):
            def tp(e, q=q):
                e.tensor_copy(out=tapbuf[:, :], in_=big(BIG0, q))
                return e.dma_start(out=taps["z"][q * 128:(q + 1) * 128, :], in_=tapbuf[:, :])
            g.step(f"tap_z{q}", "go", [ZDONE], tp, kind="d", dcls="tap")

    IWG = 208
    ensure_pieces(IWG)

    def mk_prt(tau):
        def fn(e):
            off = TOFF[tau]
            for kq in range(NQ):
                last = e.matmul(psb[7][:, 0:342], wsl(IWG, kq),
                                big(BIG0, kq, off, 342),
                                start=(kq == 0), stop=(kq == NQ - 1))
            return last
        return fn
    for tau in range(NT):
        deps = [cvn(IWG), ZDONE]
        if tau >= 1:
            deps.append(f"Elge{tau - 1}")
        g.step(f"Eprt{tau}", "pe", deps, mk_prt(tau))
        g.step(f"Elge{tau}", "ac", [f"Eprt{tau}"],
               lambda e, tau=tau: e.activation(out=lgrow(TOFF[tau], 342),
                                               in_=psb[7][0:8, 0:342], func=AF.Copy))
    use_step[IWG] = f"Eprt{NT - 1}"

    if "lgt" in taps:
        def tplg(e):
            return e.dma_start(out=taps["lgt"][:, :], in_=lgrow(0, TP))
        g.step("tap_lgt", "go", [f"Elge{NT - 1}"], tplg, kind="d", dcls="tap")

    for gr in range(NG):
        g0, gn = GRP[gr]

        def ptr(e, gr=gr, g0=g0, gn=gn):
            return e.transpose(psb[6][0:gn, 0:8], lgrow(g0, gn), eye_sb[0:8, 0:8])
        deps = [f"Elge{NT - 1}"]
        if gr >= 1:
            deps.append(f"Etm{gr - 1}")
        g.step(f"Etr{gr}", "pe", deps, ptr)

        def tmc(e, gr=gr, g0=g0, gn=gn):
            return e.tensor_copy(out=tmg(gr, 0)[0:gn, :], in_=psb[6][0:gn, 0:8])
        g.step(f"Etm{gr}", "ve", [f"Etr{gr}"], tmc)

    def sm1(e):
        for gr in range(NG):
            gn = GRP[gr][1]
            e.tensor_reduce(out=ecol(gr)[0:gn, :], in_=tmg(gr, 0)[0:gn, :], axis=AXX, op=OP.max)
        for gr in range(NG):
            gn = GRP[gr][1]
            e.tensor_scalar(out=ecol(16 + gr)[0:gn, :], in0=ecol(gr)[0:gn, :],
                            scalar1=-1.0, scalar2=None, op0=OP.mult)
        return e.engine_nop()
    g.step("Esm1", "ve", [f"Etm{NG - 1}"], sm1)

    def sme(e):
        for gr in range(NG):
            gn = GRP[gr][1]
            last = e.activation(out=tmg(gr, 1)[0:gn, :], in_=tmg(gr, 0)[0:gn, :],
                                func=AF.Exp, bias=ecol(16 + gr)[0:gn, :])
        return last
    g.step("Esme", "ac", ["Esm1"], sme)

    def sm2(e):
        for gr in range(NG):
            gn = GRP[gr][1]
            e.tensor_reduce(out=ecol(gr)[0:gn, :], in_=tmg(gr, 1)[0:gn, :], axis=AXX, op=OP.add)
        for gr in range(NG):
            gn = GRP[gr][1]
            e.reciprocal(out=ecol(16 + gr)[0:gn, :], in_=ecol(gr)[0:gn, :])
        for gr in range(NG):
            gn = GRP[gr][1]
            e.tensor_scalar(out=tmg(gr, 0)[0:gn, :], in0=tmg(gr, 1)[0:gn, :],
                            scalar1=ecol(16 + gr)[0:gn, :], scalar2=None, op0=OP.mult)
        for gr in range(NG):
            gn = GRP[gr][1]
            e.tensor_reduce(out=ecol(gr)[0:gn, :], in_=tmg(gr, 0)[0:gn, :], axis=AXX, op=OP.max)
        for gr in range(NG):
            gn = GRP[gr][1]
            e.tensor_scalar(out=tmg(gr, 1)[0:gn, :], in0=tmg(gr, 0)[0:gn, :],
                            scalar1=ecol(gr)[0:gn, :], scalar2=None, op0=OP.is_ge)
        for gr in range(NG):
            gn = GRP[gr][1]
            e.tensor_scalar(out=tmg(gr, 1)[0:gn, :], in0=tmg(gr, 1)[0:gn, :],
                            scalar1=-999.0, scalar2=999.0, op0=OP.mult, op1=OP.add)
        for gr in range(NG):
            gn = GRP[gr][1]
            e.tensor_tensor(out=tmg(gr, 1)[0:gn, :], in0=tmg(gr, 1)[0:gn, :],
                            in1=small[0:gn, 144:152], op=OP.add)
        for gr in range(NG):
            gn = GRP[gr][1]
            e.tensor_reduce(out=ecol(gr)[0:gn, :], in_=tmg(gr, 1)[0:gn, :], axis=AXX, op=OP.min)
        for gr in range(NG):
            gn = GRP[gr][1]
            e.tensor_scalar(out=tmg(gr, 1)[0:gn, :], in0=small[0:gn, 144:152],
                            scalar1=ecol(gr)[0:gn, :], scalar2=None, op0=OP.is_equal)
        for gr in range(NG):
            gn = GRP[gr][1]
            e.scalar_tensor_tensor(out=tmg(gr, 0)[0:gn, :] if False else
                                   kv_sb[0:gn, 2 * TP + gr * 8:2 * TP + gr * 8 + 8],
                                   in0=tmg(gr, 1)[0:gn, :], scalar=-1e30,
                                   in1=tmg(gr, 0)[0:gn, :], op0=OP.mult, op1=OP.add)
        for gr in range(NG):
            gn = GRP[gr][1]
            e.tensor_reduce(out=ecol(gr)[0:gn, :], in_=kv_sb[0:gn, 2 * TP + gr * 8:
                                                    2 * TP + gr * 8 + 8],
                            axis=AXX, op=OP.max)
        for gr in range(NG):
            gn = GRP[gr][1]
            e.tensor_scalar(out=kv_sb[0:gn, 2 * TP + gr * 8:2 * TP + gr * 8 + 8],
                            in0=kv_sb[0:gn, 2 * TP + gr * 8:2 * TP + gr * 8 + 8],
                            scalar1=ecol(gr)[0:gn, :], scalar2=None, op0=OP.is_ge)
        for gr in range(NG):
            gn = GRP[gr][1]
            e.tensor_tensor(out=tmg(gr, 1)[0:gn, :], in0=tmg(gr, 1)[0:gn, :],
                            in1=kv_sb[0:gn, 2 * TP + gr * 8:2 * TP + gr * 8 + 8],
                            op=OP.add)
        for gr in range(NG):
            gn = GRP[gr][1]
            last = e.tensor_tensor(out=tmg(gr, 1)[0:gn, :], in0=tmg(gr, 1)[0:gn, :],
                                   in1=tmg(gr, 0)[0:gn, :], op=OP.mult)
        return last
    g.step("Esm2", "ve", ["Esme"], sm2)

    if "wts" in taps:
        for gr in range(NG):
            g0, gn = GRP[gr]

            def tp(e, gr=gr, g0=g0, gn=gn):
                e.tensor_copy(out=tapbuf[:, 0:8][0:gn, :], in_=tmg(gr, 1)[0:gn, :])
                return e.dma_start(out=taps["wts"][g0:g0 + gn, :], in_=tapbuf[:, 0:8][0:gn, :])
            g.step(f"tap_wts{gr}", "go", ["Esm2"], tp, kind="d", dcls="tap")

    for gr in range(NG):
        g0, gn = GRP[gr]

        def ptw(e, gr=gr, g0=g0, gn=gn):
            return e.transpose(psb[6][0:8, 0:gn], tmg(gr, 1)[0:gn, 0:8],
                               eye_sb[0:gn, 0:gn])
        deps = ["Esm2"]
        if gr >= 1:
            deps.append(f"Ewf{gr - 1}")
        g.step(f"Etw{gr}", "pe", deps, ptw)

        def ewf(e, gr=gr, g0=g0, gn=gn):
            return e.tensor_copy(out=wfm_sb[0:8, g0:g0 + gn], in_=psb[6][0:8, 0:gn])
        g.step(f"Ewf{gr}", "ve", [f"Etw{gr}"], ewf)
    WFM = f"Ewf{NG - 1}"

    ensure_pieces(IWG + 2)
    for mq in range(NQ):
        ig, iu = IWG + 1 + 2 * mq, IWG + 2 + 2 * mq

        for tau in range(NT):
            i5 = mq * NT + tau

            def pg(e, ig=ig, mq=mq, tau=tau, i5=i5):
                off = TOFF[tau]
                for kq in range(NQ):
                    last = e.matmul(psb[i5 % 2][:, 0:342], wsl(ig, kq),
                                    big(BIG0, kq, off, 342),
                                    start=(kq == 0), stop=(kq == NQ - 1))
                return last
            deps = [cvn(ig), ZDONE]
            if i5 >= 2:
                deps.append(f"Egs{i5 - 2}")
            g.step(f"Epg{mq}t{tau}", "pe", deps, pg)

            def eg(e, i5=i5):
                b = i5 % 2
                return e.activation(out=sbf(i5 % 3)[:, 0:342], in_=psb[b][:, 0:342],
                                    func=AF.Silu)
            deps = [f"Epg{mq}t{tau}"]
            if i5 >= 3:
                deps.append(f"Egt{i5 - 3}")
            g.step(f"Egs{i5}", "ac", deps, eg)
        use_step[ig] = f"Epg{mq}t{NT - 1}"

        for tau in range(NT):
            i5 = mq * NT + tau

            def pu(e, iu=iu, mq=mq, tau=tau, i5=i5):
                off = TOFF[tau]
                for kq in range(NQ):
                    last = e.matmul(psb[2 + i5 % 2][:, 0:342], wsl(iu, kq),
                                    big(BIG0, kq, off, 342),
                                    start=(kq == 0), stop=(kq == NQ - 1))
                return last
            deps = [cvn(iu), ZDONE]
            if i5 >= 2:
                deps.append(f"Egt{i5 - 2}")
            g.step(f"Epu{mq}t{tau}", "pe", deps, pu)

            def pwb(e, mq=mq, tau=tau, i5=i5):
                off = TOFF[tau]
                wb_b = 4 if i5 % 2 == 0 else 6
                return e.matmul(psb[wb_b][:, 0:342], selq_sb[:, mq * 128:(mq + 1) * 128],
                                wfm_sb[0:8, off:off + 342], start=True, stop=True)
            deps = [WFM, f"Epu{mq}t{tau}"]
            if i5 >= 2:
                deps.append(f"Egt{i5 - 2}")
            g.step(f"Ewb{i5}", "pe", deps, pwb)

            def gt(e, mq=mq, tau=tau, i5=i5):
                off = TOFF[tau]
                b = 2 + i5 % 2
                wb_b = 4 if i5 % 2 == 0 else 6
                e.tensor_tensor(out=big(BIG1, mq, off, 342), in0=psb[b][:, 0:342],
                                in1=sbf(i5 % 3)[:, 0:342], op=OP.mult)
                return e.tensor_tensor(out=big(BIG1, mq, off, 342),
                                       in0=big(BIG1, mq, off, 342),
                                       in1=psb[wb_b][:, 0:342], op=OP.mult)
            g.step(f"Egt{i5}", "ve", [f"Egs{i5}", f"Ewb{i5}", KVDONE], gt)
        use_step[iu] = f"Epu{mq}t{NT - 1}"
        ensure_pieces(IWG + 2 + 2 * mq + 2)
    GATED = f"Egt{(NQ - 1) * NT + NT - 1}"

    if "gated" in taps:
        for q in range(NQ):
            def tp(e, q=q):
                e.tensor_copy(out=tapbuf[:, :], in_=big(BIG1, q))
                return e.dma_start(out=taps["gated"][q * 128:(q + 1) * 128, :], in_=tapbuf[:, :])
            g.step(f"tap_gated{q}", "go", [GATED], tp, kind="d", dcls="tap")

    ILA = IWG + 33
    ensure_pieces(ILA)

    for tau in range(NT):
        def plr(e, tau=tau):
            off = TOFF[tau]
            for kq in range(NQ):
                last = e.matmul(psb[5][:, 0:342], wsl(ILA, kq),
                                big(BIG0, kq, off, 342),
                                start=(kq == 0), stop=(kq == NQ - 1))
            return last
        deps = [cvn(ILA), ZDONE]
        if tau >= 1:
            deps.append(f"Elr{tau - 1}")
        g.step(f"Eplr{tau}", "pe", deps, plr)

        def pw16(e, tau=tau):
            off = TOFF[tau]
            return e.matmul(psb[6][:, 0:342], sel16_sb[:, :],
                            wfm_sb[0:8, off:off + 342], start=True, stop=True)
        deps = [WFM, f"Etw{NG-1}", GATED]
        if tau >= 1:
            deps.append(f"Elr{tau - 1}")
        g.step(f"Ew16{tau}", "pe", deps, pw16)

        def lrf(e, tau=tau):
            off = TOFF[tau]
            e.tensor_copy(out=S(3, off, 342), in_=psb[6][:, 0:342])
            return e.tensor_tensor(out=BIG2[:, 15 * TP + off:15 * TP + off + 342],
                                   in0=S(3, off, 342), in1=psb[5][:, 0:342],
                                   op=OP.mult)
        g.step(f"Elr{tau}", "ve", [f"Eplr{tau}", f"Ew16{tau}", KVDONE], lrf)
    use_step[ILA] = f"Eplr{NT - 1}"
    LRB = f"Elr{NT - 1}"
    lrb_bf = BIG2[:, 15 * TP:16 * TP]

    ensure_pieces(ILA + 2)
    for mq in range(NQ):
        idn, ib = ILA + 1 + 2 * mq, ILA + 2 + 2 * mq

        for tau in range(NT):
            i6 = mq * NT + tau

            def pd(e, idn=idn, ib=ib, mq=mq, tau=tau, i6=i6):
                off = TOFF[tau]
                b = i6 % 2
                for kq in range(NQ):
                    e.matmul(psb[b][:, 0:342], wsl(idn, kq), big(BIG1, kq, off, 342),
                             start=(kq == 0), stop=False)
                last = e.matmul(psb[b][:, 0:342], wslot(ib)[:, 0:128],
                                lrb_bf[:, off:off + 342], start=False, stop=True)
                return last
            deps = [cvn(idn), cvn(ib), GATED, LRB]
            if i6 >= 2:
                deps.append(f"Efo{i6 - 2}")
            g.step(f"Epd{mq}t{tau}", "pe", deps, pd)
        use_step[idn] = f"Epd{mq}t{NT - 1}"
        use_step[ib] = f"Epd{mq}t{NT - 1}"

        for tau in range(NT):
            i6 = mq * NT + tau

            def efo(e, mq=mq, tau=tau, i6=i6):
                b = i6 % 2
                e.activation(out=big(BIG2, mq, TOFF[tau], 342), in_=psb[b][:, 0:342],
                             func=AF.Copy)
                return e.activation(out=sbf(3 + i6 % 3)[:, 0:342], in_=psb[b][:, 0:342],
                                    func=AF.Square)
            deps = [f"Epd{mq}t{tau}"]
            if i6 >= 3:
                deps.append(f"Est5_{i6 - 3}")
            if mq == 0 and tau == 0:
                deps.append(LRB)
            g.step(f"Efo{i6}", "ac", deps, efo)

            def st5(e, mq=mq, tau=tau, i6=i6):
                e.matmul(psb[2 + tau][:, 0:342], ones_bf[:, :],
                         big(BIG2, mq, TOFF[tau], 342),
                         start=(mq == 0), stop=(mq == NQ - 1))
                return e.matmul(psb[5 + tau][:, 0:342] if tau < 3 else None,
                                ones_bf[:, :], sbf(3 + i6 % 3)[:, 0:342],
                                start=(mq == 0), stop=(mq == NQ - 1))
            deps = [f"Efo{i6}"]
            if mq == 0:
                deps.append(f"Cst4_{(NQ - 1) * NT + tau}")
            g.step(f"Est5_{i6}", "pe", deps, st5)
        if True:
            ensure_pieces(ILA + 2 + 2 * mq + 2)


    if "fo" in taps:
        for q in range(NQ):
            def tp(e, q=q):
                e.tensor_copy(out=tapbuf[:, :], in_=big(BIG2, q))
                return e.dma_start(out=taps["fo"][q * 128:(q + 1) * 128, :], in_=tapbuf[:, :])
            g.step(f"tap_fo{q}", "go", [f"Est5_{(NQ - 1) * NT + NT - 1}"], tp,
                   kind="d", dcls="tap")

    for tau in range(NT):
        def l5v(e, tau=tau):
            e.tensor_scalar(out=P(0, tau * 342, 342), in0=psb[2 + tau][:, 0:342],
                            scalar1=RC, scalar2=None, op0=OP.mult)
            e.tensor_tensor(out=P(2, 0, 342), in0=P(0, tau * 342, 342),
                            in1=P(0, tau * 342, 342), op=OP.mult)
            return e.scalar_tensor_tensor(out=P(2, 342, 342), in0=psb[5 + tau][:, 0:342],
                                          scalar=RC, in1=P(2, 0, 342),
                                          op0=OP.mult, op1=OP.subtract)
        g.step(f"El5v{tau}", "ve", [f"Est5_{(NQ - 1) * NT + tau}"], l5v)

        g.step(f"El5a{tau}", "ac", [f"El5v{tau}"],
               lambda e: e.activation(out=P(2, 684, 342), in_=P(2, 342, 342),
                                      func=AF.Sqrt, bias=eps_col[:, 0:1]))

        def l5r(e, tau=tau):
            return e.reciprocal(out=P(1, tau * 342, 342), in_=P(2, 684, 342))
        g.step(f"El5r{tau}", "ve", [f"El5a{tau}"], l5r)
    LN5 = f"El5r{NT - 1}"

    for mq in range(NQ):
        def X1b4(e, mq=mq):
            ap = bass.AP(sp_X1.ap().tensor, mq * 128 * TP, [[TP, 128], [1, TP]])
            return e.dma_start(out=zring(mq), in_=ap)
        deps = [LN5]
        if mq >= 2:
            deps.append(f"EX2_{mq - 2}")
        g.step(f"EX1b{mq}", "sy", deps, X1b4, kind="d", dcls=f"z{mq % 2}")

        def X2f(e, mq=mq):
            for tau in range(NT):
                off = TOFF[tau]
                e.tensor_tensor(out=P(3, off, 342), in0=big(BIG2, mq, off, 342),
                                in1=P(0, tau * 342, 342), op=OP.subtract)
                e.tensor_tensor(out=P(3, off, 342), in0=P(3, off, 342),
                                in1=P(1, tau * 342, 342), op=OP.mult)
                last = e.tensor_tensor(out=P(3, off, 342), in0=P(3, off, 342),
                                       in1=zring(mq, off, 342), op=OP.add)
            return last
        deps = [f"EX1b{mq}", LN5]
        if mq >= 1:
            deps.append(f"Eout{mq - 1}")
        g.step(f"EX2_{mq}", "ve", deps, X2f)

        def outd(e, mq=mq):
            return e.dma_start(out=out[mq * 128:(mq + 1) * 128, :], in_=P(3, 0, TP))
        g.step(f"Eout{mq}", "go", [f"EX2_{mq}"], outd, kind="d", dcls="outd")

    def fin(e):
        for cname in sorted(g.dcls):
            sem_, tot = g.dcls[cname]
            if tot:
                e.wait_ge(sem_, tot)
        return e.engine_nop()
    g.step("fin", "go", [], fin)

    if TRUNCATE is not None or KEEP_DROP:
        g.truncate(TRUNCATE if TRUNCATE is not None else 10 ** 9, KEEP_DROP)
    blk = es.enter_context(nc.Block())
    g.emit(blk)
    build_program._sb = {t.tensor.name if hasattr(t, "tensor") else None: t
                         for t in []}
    build_program._handles = dict(wbf0=wbf[0], wbf1=wbf[1], wbf2=wbf[2], wstage0=wstage[0], wstage1=wstage[1], kv_sb_=kv_sb, mu_full=mu_full, r0_full=r0_full, rq_full=rq_full, h_t=h_t, xring=xring,
                                  BIG0=BIG0, BIG1=BIG1, BIG2=BIG2, hlast=hlast,
                                  kv_sb=kv_sb, s_pad=s_pad, a_pad=a_pad, b_pad=b_pad,
                                  summ_sb=summ_sb, carry_sb=carry_sb, pool=pool,
                                  vec_sb=vec_sb, wfm_sb=wfm_sb, ndst=ndst)
    es.close()
    return nc, taps


def _stage_inputs(inputs):
    p = {k[8:] if k.startswith("L0_attn_") else k: np.asarray(v)
         for k, v in inputs.items()}
    x2d = np.asarray(p["x"][0], np.float32)
    xT = np.ascontiguousarray(x2d.T)

    staged = {}
    staged["Wk"] = _stage_w(p["Wk"])
    staged["Wv"] = _stage_w(p["Wv"])
    staged["Wr"] = _stage_w(p["Wr"])
    staged["Wo"] = _stage_w(p["Wo"])
    staged["Wck"] = _stage_w(p["Wck"])
    staged["Wcv"] = _stage_w(p["Wcv"])
    staged["Wcr"] = _stage_w(p["Wcr"])
    staged["Wgate"] = _stage_w(np.asarray(p["L0_ffn_gate_w"]).reshape(2048, 2048))
    staged["Wup"] = _stage_w(np.asarray(p["L0_ffn_up_w"]).reshape(2048, 2048))
    staged["Wdown"] = _stage_w(np.asarray(p["L0_ffn_down_w"]).transpose(1, 0, 2)
                               .reshape(2048, 2048))
    staged["WlA"] = _stage_w(np.asarray(p["L0_ffn_lora_A"]).reshape(128, 2048))
    staged["WlB"] = _stage_w(np.asarray(p["L0_ffn_lora_B"]).transpose(1, 0, 2)
                             .reshape(2048, 128))
    WGp = np.zeros((128, 2048), np.float32)
    WGp[0:8, :] = np.asarray(p["L0_ffn_G"], np.float32).T
    staged["WG"] = _stage_w(WGp)

    vec_list = [p["ln1_w"], p["ln1_b"], p["ln2_w"], p["ln2_b"],
                np.asarray(p["tm_k"]).ravel(), np.asarray(p["tm_v"]).ravel(),
                np.asarray(p["tm_r"]).ravel(), np.asarray(p["cm_k"]).ravel(),
                np.asarray(p["cm_r"]).ravel(), p["time_decay"], p["time_first"]]
    vecs = np.concatenate([_stage_vec(v) for v in vec_list], axis=1)

    eye = np.eye(128, dtype=np.float32)
    sel16 = np.zeros((8, 128), np.float32)
    for e_ in range(8):
        sel16[e_, e_ * 16:(e_ + 1) * 16] = 1.0
    selq = np.zeros((8, NQ * 128), np.float32)
    for q in range(NQ):
        selq[q // 2, q * 128:(q + 1) * 128] = 1.0

    in_maps = []
    for c in range(NCORES):
        t0 = c * TLOC
        xs = np.zeros((C, XW), np.float32)
        lo = t0 - 1
        hi = t0 + XW - 1
        slo, shi = max(lo, 0), min(hi, T)
        xs[:, slo - lo:shi - lo] = xT[:, slo:shi]
        sh = np.zeros((128, 8), np.float32)
        mk = np.zeros((128, 8), np.float32)
        for i in range(8):
            if i < c:
                sh[:, i] = 1024.0 * (c - 1 - i)
                mk[:, i] = 1.0
            else:
                sh[:, i] = 1.0e30
        hm = np.full((128, 1), 1.0 if c > 0 else 0.0, np.float32)
        m = {"xf": xs, "vecs": vecs, "shifts": sh, "masks": mk, "halom": hm,
             "eyed": eye, "sel16": sel16, "selq": selq}
        m.update(staged)
        in_maps.append(m)
    return in_maps


_PROG_CACHE = {}


def kernel(**inputs):
    key = (tuple(sorted(DEBUG_TAPS)), TRUNCATE)
    if key not in _PROG_CACHE:
        _PROG_CACHE[key] = build_program(DEBUG_TAPS)
    nc, taps = _PROG_CACHE[key]
    in_maps = _stage_inputs(inputs)
    res = run_bass_kernel_spmd(nc, in_maps, core_ids=list(range(NCORES)))
    full = np.empty((T, C), np.float32)
    for c in range(NCORES):
        t0 = c * TLOC
        oc = np.asarray(res.results[c]["out"])
        j0 = 0 if c == 0 else 1
        j1 = min(TP, T - t0)
        if j1 > 1025:
            j1 = 1025
        full[t0 + j0:t0 + j1, :] = oc[:, j0:j1].T
    kernel._last = res
    return full.reshape(1, T, C)
